# version 01
# baseline (speedup 1.0000x reference)
import contextlib
import numpy as np
import ml_dtypes
import concourse.bass as bass
import concourse.mybir as mybir
from concourse.bass_utils import run_bass_kernel_spmd
from concourse.alu_op_type import AluOpType as OP

F32 = mybir.dt.float32
BF16 = mybir.dt.bfloat16
AF = mybir.ActivationFunctionType
AX = mybir.AxisListType

B, T, D = 4, 4096, 1024
T2 = T // 2
HALO = 4
T2H = T2 + HALO
E = 2048
NST = 16
H, DK, DV = 8, 64, 64
BF = ml_dtypes.bfloat16
ENGS = ("sync", "gpsimd", "scalar", "vector", "tensor")


def chunks(start, total, step):
    out, o = [], start
    while o < start + total:
        w = min(step, start + total - o)
        out.append((o, w))
        o += w
    return out


class OpRec:
    __slots__ = ("fn", "incs", "chain_val")

    def __init__(self, fn):
        self.fn = fn
        self.incs = []
        self.chain_val = None


class Prog:

    CHAINED = ("vector", "scalar", "gpsimd", "tensor")

    def __init__(self, name):
        self.nc = bass.Bass("TRN2", target_bir_lowering=False)
        self.es = contextlib.ExitStack()
        self.name = name
        self.sems = {}
        self.q = {e: [] for e in ENGS}
        self.last = {}
        self.acc = {}
        self.chain_cnt = {e: 0 for e in self.CHAINED}
        self.pending = {e: [] for e in self.CHAINED}
        self.wait_hwm = {}
        for e in self.CHAINED:
            self.sem("CH_" + e)

    def sem(self, name):
        self.sems[name] = [self.es.enter_context(self.nc.semaphore(name)), 0]

    def sb(self, name, shape, dtype):
        cm = self.nc.sbuf_tensor(name, shape, dtype)
        h = cm.__enter__()
        if not hasattr(self, "_cms"):
            self._cms = {}
        self._cms[name] = cm
        return h

    def sb_free(self, *names):
        for n in names:
            cm = self._cms.pop(n)
            cm.__exit__(None, None, None)

    def ps(self, name, shape, dtype=F32):
        return self.es.enter_context(self.nc.psum_tensor(name, shape, dtype))

    @staticmethod
    def _span(ap):
        try:
            lo = ap.offset
            ext = 1
            for (step, count) in list(ap.ap):
                ext += abs(step) * (count - 1)
            return (ap.tensor.name, lo, lo + ext)
        except Exception:
            return (getattr(getattr(ap, "tensor", None), "name", "?"), 0, 1 << 60)

    def _sync_to(self, en, oen, rec):
        name = "CH_" + oen
        if getattr(rec, "chain_val", None) is None:
            newp = []
            found = False
            for r in self.pending[oen]:
                if not found:
                    self.chain_cnt[oen] += 1
                    r.incs.append((self.sems[name][0], 1))
                    r.chain_val = self.chain_cnt[oen]
                    self.sems[name][1] = self.chain_cnt[oen]
                    if r is rec:
                        found = True
                else:
                    newp.append(r)
            self.pending[oen] = newp
            assert found, "rec not in pending list"
        hw = self.wait_hwm.setdefault(en, {}).get(oen, 0)
        if rec.chain_val > hw:
            self.wait_hwm[en][oen] = rec.chain_val
            self.wait(en, name, rec.chain_val)

    def _dep(self, en, reads, writes):
        spans = [(self._span(a), False) for a in reads] + [(self._span(a), True) for a in writes]
        need = []
        for (nm, lo, hi), is_w in spans:
            for (oen, orec, olo, ohi, owrite) in self.acc.get(nm, ()):
                if hi <= olo or lo >= ohi:
                    continue
                if not (is_w or owrite):
                    continue
                if oen == en == "tensor":
                    continue
                need.append((oen, orec))
        for oen, orec in need:
            self._sync_to(en, oen, orec)
        return spans

    def _register(self, en, rec, spans):
        for (nm, lo, hi), is_w in spans:
            self.acc.setdefault(nm, []).append((en, rec, lo, hi, is_w))

    def _rec(self, en, fn, incable=True):
        r = OpRec(fn)
        r.chain_val = None
        self.q[en].append(r)
        if incable:
            self.last[en] = r
            if en in self.CHAINED:
                self.pending[en].append(r)
        return r

    def raw(self, en, method, *args, reads=(), writes=(), **kw):
        spans = self._dep(en, list(reads), list(writes))
        def fn(e, r):
            instr = getattr(e, method)(*args, **kw)
            for i, (s, k) in enumerate(r.incs):
                if i == 0:
                    instr.then_inc(s, k)
                else:
                    e.nop().then_inc(s, k)
            return instr
        r = self._rec(en, fn)
        self._register(en, r, spans)
        return r

    def wait(self, en, name, val=None):
        s = self.sems[name]
        v = s[1] if val is None else val
        def fn(e, r):
            e.wait_ge(s[0], v)
        self._rec(en, fn, incable=False)

    def inc_last(self, en, name, k=1):
        s = self.sems[name]
        s[1] += k
        r = self.last.get(en)
        if r is not None:
            r.incs.append((s[0], k))
        else:
            def fn(e, _r):
                e.sem_inc(s[0], k)
            self._rec(en, fn, incable=False)

    def tgt(self, name):
        return self.sems[name][1]

    def dma(self, en, out, in_, sem, slow=False):
        s = self.sems[sem]
        s[1] += 16
        def fn(e, r):
            instr = e.dma_start(out=out, in_=in_, allow_slow_non_contiguous=True) if slow \
                else e.dma_start(out=out, in_=in_)
            instr.then_inc(s[0], 16)
        self._rec(en, fn, incable=False)
        return s[1]

    def mm(self, out, lhsT, rhs, **kw):
        return self.raw("tensor", "matmul", out, lhsT, rhs, reads=(lhsT, rhs), writes=(out,), **kw)

    def trans(self, out, in_, ident, **kw):
        return self.raw("tensor", "matmul", out, in_, ident, start=True, stop=True,
                        is_transpose=True, reads=(in_, ident), writes=(out,), **kw)

    def act(self, out, in_, func, bias=None, scale=None, accum_out=None):
        kw = {}
        rd = [in_]
        wr = [out]
        if bias is not None:
            kw["bias"] = bias
            if not isinstance(bias, (int, float)):
                rd.append(bias)
        if scale is not None:
            kw["scale"] = scale
            if not isinstance(scale, (int, float)):
                rd.append(scale)
        if accum_out is not None:
            kw["accum_out"] = accum_out
            wr.append(accum_out)
        return self.raw("scalar", "activation", out, in_, func, reads=rd, writes=wr, **kw)

    def tt(self, out, a, b, op):
        return self.raw("vector", "tensor_tensor", out, a, b, op, reads=(a, b), writes=(out,))

    def ts(self, out, in_, s1, s2=None, op0=OP.mult, op1=OP.add):
        rd = [in_] + [s for s in (s1, s2) if s is not None and not isinstance(s, (int, float))]
        if s2 is None:
            return self.raw("vector", "tensor_scalar", out, in_, s1, None, op0, reads=rd, writes=(out,))
        return self.raw("vector", "tensor_scalar", out, in_, s1, s2, op0, op1, reads=rd, writes=(out,))

    def stt(self, out, in0, scalar, in1, op0=OP.mult, op1=OP.add):
        rd = [in0, in1] + ([scalar] if not isinstance(scalar, (int, float)) else [])
        return self.raw("vector", "scalar_tensor_tensor", out, in0, scalar, in1, op0, op1,
                        reads=rd, writes=(out,))

    def vcopy(self, out, in_):
        return self.raw("vector", "tensor_copy", out, in_, reads=(in_,), writes=(out,))

    def vrecip(self, out, in_):
        return self.raw("vector", "reciprocal", out, in_, reads=(in_,), writes=(out,))

    def vscan(self, out, a, u, init):
        rd = [a, u] + ([init] if not isinstance(init, (int, float)) else [])
        return self.raw("vector", "tensor_tensor_scan", out, a, u, init, OP.mult, OP.add,
                        reads=rd, writes=(out,))

    def vreduce(self, out, in_, op=OP.max, absval=None):
        return self.raw("vector", "tensor_reduce", out, in_, AX.X, op, apply_absolute_value=absval,
                        reads=(in_,), writes=(out,))

    def vselect(self, out, mask, on_true, on_false):
        return self.raw("vector", "select", out, mask, on_true, on_false,
                        reads=(mask, on_true, on_false), writes=(out,))

    def memset(self, en, ap, val):
        return self.raw(en, "memset", ap, val, reads=(), writes=(ap,))

    def barrier(self, engines=ENGS):
        engines = ENGS
        if "BAR" not in self.sems:
            self.sem("BAR")
        bar = self.sems["BAR"]
        for en in engines:
            self.inc_last(en, "BAR", 1)
        self.last = {}
        self.acc = {}
        self.wait_hwm = {}
        v = bar[1]
        for en in engines:
            self.wait(en, "BAR", v)

    def build(self):
        block = self.es.enter_context(self.nc.Block())
        for en in ENGS:
            ops = self.q[en]
            dec = getattr(block, en)
            def make(ops_):
                def body(e):
                    for r in ops_:
                        r.fn(e, r)
                return body
            dec(make(ops))
        self.es.close()
        return self.nc



def build_stage1(phases='ABCD', a_stop=None, na_chunks=None):
    p = Prog("s1")
    nc = p.nc
    xT = nc.dram_tensor("xT", [D, T2H], F32, kind="ExternalInput").ap()
    in_wT = nc.dram_tensor("in_wT", [D, E], F32, kind="ExternalInput").ap()
    skip_wT = nc.dram_tensor("skip_wT", [D, E], F32, kind="ExternalInput").ap()
    wpackT = nc.dram_tensor("wpackT", [E, 96], F32, kind="ExternalInput").ap()
    wd2T = nc.dram_tensor("wd2T", [64, E], F32, kind="ExternalInput").ap()
    conv_w = nc.dram_tensor("conv_w", [E, 4], F32, kind="ExternalInput").ap()
    conv_b = nc.dram_tensor("conv_b", [E, 1], F32, kind="ExternalInput").ap()
    wd2_b = nc.dram_tensor("wd2_b", [E, 1], F32, kind="ExternalInput").ap()
    norm_w = nc.dram_tensor("norm_w", [D, 1], F32, kind="ExternalInput").ap()
    ones_col = nc.dram_tensor("ones_col", [128, 1], F32, kind="ExternalInput").ap()
    ones_row = nc.dram_tensor("ones_row", [1, 128], F32, kind="ExternalInput").ap()

    xn_out = nc.dram_tensor("xn_out", [D, T2], F32, kind="ExternalOutput").ap()
    xs_out = nc.dram_tensor("xs_out", [E, T2], BF16, kind="ExternalOutput").ap()
    delta_out = nc.dram_tensor("delta_out", [E, T2], BF16, kind="ExternalOutput").ap()
    skip_out = nc.dram_tensor("skip_out", [E, T2], F32, kind="ExternalOutput").ap()
    bc_out = nc.dram_tensor("bc_out", [32, T2], F32, kind="ExternalOutput").ap()

    DT, ET = D // 128, E // 128
    CH = chunks(0, T2H, 512)
    CO = chunks(0, T2, 512)

    x_sb = [p.sb(f"x{i}", [128, T2H], F32) for i in range(DT)]
    xn2_sb = [p.sb(f"xn2_{i}", [128, T2H], BF16) for i in range(DT)]
    sq2 = [p.sb(f"sq2_{i}", [128, 512], F32) for i in range(2)]
    tmp_sb = p.sb("tmpc", [128, 512], F32)
    inwb = [p.sb(f"inwb{i}", [128, 8 * 128], BF16) for i in range(2)]
    skwb = [p.sb(f"skwb{i}", [128, 8 * 128], BF16) for i in range(2)]
    wpk_sb = [p.sb(f"wpk{i}", [128, 96], BF16) for i in range(ET)]
    wd2_sb = p.sb("wd2", [64, E], BF16)
    cw_sb = p.sb("cw", [128, ET, 4], F32)
    cb_sb = p.sb("cb", [128, ET, 1], F32)
    db_sb = p.sb("db", [128, ET, 1], F32)
    nw_sb = p.sb("nw", [128, DT, 1], F32)
    ones_sb = p.sb("ones", [128, 1], F32)
    onesr_sb = p.sb("onesr", [1, 128], F32)
    mean_sb = p.sb("meanr", [1, T2H], F32)
    rstd_sb = p.sb("rstdr", [1, T2H], F32)
    trow_sb = p.sb("trow", [1, T2H], F32)
    xin_sb = p.sb("xin", [128, T2H], F32)
    xc_sb = p.sb("xc", [128, T2], F32)
    xs_sb = [p.sb(f"xsb{i}", [128, T2], BF16) for i in range(2)]
    skipb_sb = [p.sb(f"skipb{i}", [128, T2], F32) for i in range(2)]
    xhat_sb = p.sb("xhat", [96, T2], F32)
    xhatb_sb = p.sb("xhatb", [64, T2], BF16)
    e1_sb = p.sb("e1", [128, 512], F32)
    dl_sb = [p.sb(f"dl{i}", [128, T2], BF16) for i in range(2)]

    PS = [p.ps(f"ps{i}", [128, 512], F32) for i in range(8)]

    cst = p.sb("cst", [128, 4], F32)
    for s in ("LD", "LD2", "LDW", "CONS", "ST"):
        p.sem(s)
    p.memset("vector", cst[:, 0:1], 1e-5)
    p.memset("vector", cst[:, 1:2], 1.0)

    for i in range(DT):
        p.dma("sync", x_sb[i][:], xT[i * 128:(i + 1) * 128, :], "LD")
    p.dma("sync", ones_sb[:], ones_col[:, :], "LD")
    p.dma("sync", onesr_sb[:], ones_row[:, :], "LD")
    for i in range(ET):
        p.dma("gpsimd", wpk_sb[i][:], wpackT[i * 128:(i + 1) * 128, :], "LD2")
    p.dma("gpsimd", wd2_sb[:], wd2T[:, :], "LD2")
    p.dma("gpsimd", cw_sb[:], conv_w.rearrange("(a p) k -> p a k", p=128)[:, :, :], "LD2")
    p.dma("gpsimd", cb_sb[:], conv_b.rearrange("(a p) k -> p a k", p=128)[:, :, :], "LD2", slow=True)
    p.dma("gpsimd", db_sb[:], wd2_b.rearrange("(a p) k -> p a k", p=128)[:, :, :], "LD2", slow=True)
    p.dma("gpsimd", nw_sb[:], norm_w.rearrange("(a p) k -> p a k", p=128)[:, :, :], "LD2", slow=True)
    for en in ("scalar", "vector", "tensor"):
        p.wait(en, "LD")
        p.wait(en, "LD2")

    def col(t3, i, j):
        return t3[:, i, j:j + 1]

    if 'A' not in phases:
        p.wait('sync', 'ST')
        return p.build()
    for (o, w) in (CH if na_chunks is None else CH[:na_chunks]):
        for i in range(DT):
            p.act(sq2[i % 2][:, 0:w], x_sb[i][:, o:o + w], AF.Square)
            p.barrier(("scalar", "tensor"))
            p.mm(PS[6][0:1, 0:w], ones_sb[:], x_sb[i][:, o:o + w], start=(i == 0), stop=(i == DT - 1),
                 skip_group_check=True)
            p.mm(PS[7][0:1, 0:w], ones_sb[:], sq2[i % 2][:, 0:w], start=(i == 0), stop=(i == DT - 1),
                 skip_group_check=True)
        p.barrier(("tensor", "vector"))
        if a_stop == 'stats':
            continue
        p.ts(mean_sb[0:1, o:o + w], PS[6][0:1, 0:w], 1.0 / D)
        p.ts(rstd_sb[0:1, o:o + w], PS[7][0:1, 0:w], 1.0 / D)
        p.barrier(("vector", "scalar"))
        p.act(trow_sb[0:1, o:o + w], mean_sb[0:1, o:o + w], AF.Square)
        p.barrier(("scalar", "vector"))
        p.tt(rstd_sb[0:1, o:o + w], rstd_sb[0:1, o:o + w], trow_sb[0:1, o:o + w], OP.subtract)
        p.barrier(("vector", "scalar"))
        p.act(trow_sb[0:1, o:o + w], rstd_sb[0:1, o:o + w], AF.Sqrt, bias=cst[0:1, 0:1])
        p.barrier(("scalar", "vector"))
        p.vrecip(rstd_sb[0:1, o:o + w], trow_sb[0:1, o:o + w])
        p.barrier(("vector", "tensor"))
        if a_stop == 'rows':
            continue
        p.mm(PS[0][:, 0:w], onesr_sb[:], mean_sb[0:1, o:o + w], start=True, stop=True)
        p.mm(PS[1][:, 0:w], onesr_sb[:], rstd_sb[0:1, o:o + w], start=True, stop=True)
        p.barrier(("tensor", "vector"))
        if a_stop == 'bcast':
            continue
        for i in range(DT):
            p.tt(x_sb[i][:, o:o + w], x_sb[i][:, o:o + w], PS[0][:, 0:w], OP.subtract)
        for i in range(DT):
            p.tt(x_sb[i][:, o:o + w], x_sb[i][:, o:o + w], PS[1][:, 0:w], OP.mult)
        p.barrier(("vector", "tensor", "sync", "scalar"))

    for i in range(DT):
        p.dma("sync", xn_out[i * 128:(i + 1) * 128, :], x_sb[i][:, HALO:T2H], "ST")

    if 'B' not in phases:
        p.wait('sync', 'ST')
        return p.build()
    for (o, w) in CH:
        for i in range(DT):
            p.act(sq2[i % 2][:, 0:w], x_sb[i][:, o:o + w], AF.Square)
            p.barrier(("scalar", "tensor"))
            p.mm(PS[7][0:1, 0:w], ones_sb[:], sq2[i % 2][:, 0:w], start=(i == 0), stop=(i == DT - 1),
                 skip_group_check=True)
        p.barrier(("tensor", "vector"))
        p.ts(rstd_sb[0:1, o:o + w], PS[7][0:1, 0:w], 1.0 / D)
        p.barrier(("vector", "scalar"))
        p.act(trow_sb[0:1, o:o + w], rstd_sb[0:1, o:o + w], AF.Sqrt, bias=cst[0:1, 0:1])
        p.barrier(("scalar", "vector"))
        p.vrecip(rstd_sb[0:1, o:o + w], trow_sb[0:1, o:o + w])
        p.barrier(("vector", "tensor"))
        p.mm(PS[1][:, 0:w], onesr_sb[:], rstd_sb[0:1, o:o + w], start=True, stop=True)
        p.barrier(("tensor", "vector"))
        for i in range(DT):
            p.tt(tmp_sb[:, 0:w], x_sb[i][:, o:o + w], PS[1][:, 0:w], OP.mult)
            p.ts(xn2_sb[i][:, o:o + w], tmp_sb[:, 0:w], col(nw_sb, i, 0), None, OP.mult)
        p.barrier(("vector", "tensor", "scalar"))

    if 'C' not in phases:
        p.wait('sync', 'ST')
        return p.build()
    p.memset("vector", xhat_sb[:], 0.0)

    def load_wslices(et, buf):
        for kk in range(DT):
            p.dma("gpsimd", inwb[buf][:, kk * 128:(kk + 1) * 128],
                  in_wT[kk * 128:(kk + 1) * 128, et * 128:(et + 1) * 128], "LDW")
            p.dma("gpsimd", skwb[buf][:, kk * 128:(kk + 1) * 128],
                  skip_wT[kk * 128:(kk + 1) * 128, et * 128:(et + 1) * 128], "LDW")

    load_wslices(0, 0)
    p.barrier(("vector", "tensor"))
    for et in range(ET):
        buf = et % 2
        p.wait("tensor", "LDW", 16 * 16 * (et + 1))
        p.barrier()
        if et + 1 < ET:
            if et >= 1:
                p.wait("gpsimd", "CONS", et)
            load_wslices(et + 1, buf ^ 1)
        for (o, w) in CH:
            for kk in range(DT):
                p.mm(PS[2][:, 0:w], inwb[buf][:, kk * 128:(kk + 1) * 128], xn2_sb[kk][:, o:o + w],
                     start=(kk == 0), stop=(kk == DT - 1))
            p.barrier(("tensor", "vector"))
            p.vcopy(xin_sb[:, o:o + w], PS[2][:, 0:w])
            p.barrier(("vector", "tensor"))
        for (o, w) in CO:
            for kk in range(DT):
                p.mm(PS[4][:, 0:w], skwb[buf][:, kk * 128:(kk + 1) * 128], xn2_sb[kk][:, HALO + o:HALO + o + w],
                     start=(kk == 0), stop=(kk == DT - 1))
            p.barrier(("tensor", "vector"))
            if et >= 2 and o == 0:
                p.wait("vector", "ST", p.tgt("ST"))
            p.vcopy(skipb_sb[buf][:, o:o + w], PS[4][:, 0:w])
            p.barrier(("vector", "tensor"))
        p.inc_last("tensor", "CONS", 1)
        p.dma("sync", skip_out[et * 128:(et + 1) * 128, :], skipb_sb[buf][:], "ST")
        p.ts(xc_sb[:], xin_sb[:, 4:4 + T2], col(cw_sb, et, 3), col(cb_sb, et, 0), OP.mult, OP.add)
        for kk in range(3):
            p.stt(xc_sb[:], xin_sb[:, 1 + kk:1 + kk + T2], col(cw_sb, et, kk), xc_sb[:], OP.mult, OP.add)
        if et >= 2:
            p.wait("scalar", "ST", p.tgt("ST"))
        p.barrier(("vector", "scalar"))
        p.act(xs_sb[buf][:], xc_sb[:], AF.Silu)
        p.barrier(("scalar", "sync", "tensor"))
        p.dma("sync", xs_out[et * 128:(et + 1) * 128, :], xs_sb[buf][:], "ST")
        for (o, w) in CO:
            p.mm(PS[3][0:96, 0:w], wpk_sb[et][:], xs_sb[buf][:, o:o + w], start=True, stop=True)
            p.barrier(("tensor", "vector"))
            p.tt(xhat_sb[:, o:o + w], xhat_sb[:, o:o + w], PS[3][0:96, 0:w], OP.add)
            p.barrier(("vector", "tensor"))

    if 'D' not in phases:
        p.wait('sync', 'ST')
        return p.build()
    p.barrier(("vector", "sync", "scalar", "tensor"))
    p.dma("sync", bc_out[:, :], xhat_sb[64:96, :], "ST")
    p.vcopy(xhatb_sb[:], xhat_sb[0:64, :])
    p.barrier(("vector", "tensor", "scalar"))
    for et in range(ET):
        buf = et % 2
        if et >= 2:
            p.wait("scalar", "ST", p.tgt("ST"))
        for (o, w) in CO:
            p.mm(PS[5][:, 0:w], wd2_sb[:, et * 128:(et + 1) * 128], xhatb_sb[:, o:o + w], start=True, stop=True)
            p.barrier(("tensor", "scalar"))
            p.act(e1_sb[:, 0:w], PS[5][:, 0:w], AF.Exp, bias=col(db_sb, et, 0))
            p.act(dl_sb[buf][:, o:o + w], e1_sb[:, 0:w], AF.Ln, bias=cst[:, 1:2])
            p.barrier(("scalar", "tensor", "sync"))
        p.dma("sync", delta_out[et * 128:(et + 1) * 128, :], dl_sb[buf][:], "ST")

    p.wait("sync", "ST")
    return p.build()



def build_stage2():
    p = Prog("s2")
    nc = p.nc
    EH = E // 2
    ET2 = EH // 128
    TC = 512
    NTC = T // TC

    delta_s = nc.dram_tensor("delta_s", [EH, T], BF16, kind="ExternalInput").ap()
    xs_s = nc.dram_tensor("xs_s", [EH, T], BF16, kind="ExternalInput").ap()
    skip_s = nc.dram_tensor("skip_s", [EH, T], F32, kind="ExternalInput").ap()
    bc3 = nc.dram_tensor("bc3", [1, 32, T], F32, kind="ExternalInput").ap()
    alog = nc.dram_tensor("alog", [EH, 16], F32, kind="ExternalInput").ap()
    wdv = nc.dram_tensor("wdv", [EH, 1], F32, kind="ExternalInput").ap()
    yf_out = nc.dram_tensor("yf_out", [EH, T], F32, kind="ExternalOutput").ap()

    dl_sb = p.sb("dl", [128, TC], BF16)
    xs_sb = p.sb("xs", [128, TC], BF16)
    skp_sb = p.sb("skp", [128, TC], F32)
    silu_sb = p.sb("silu", [128, TC], F32)
    v_sb = p.sb("v", [128, 1, TC], BF16)
    a_sb = p.sb("a", [128, NST, TC], F32)
    u_sb = p.sb("u", [128, NST, TC], BF16)
    h_sb = p.sb("h", [128, NST, TC], BF16)
    brep_sb = p.sb("brep", [128, NST, TC], BF16)
    crep_sb = p.sb("crep", [128, NST, TC], BF16)
    na_sb = p.sb("na", [128, ET2, 16], F32)
    wd_sb = p.sb("wd", [128, ET2, 1], F32)
    hst_sb = p.sb("hst", [128, ET2, NST], F32)
    yf_sb = p.sb("yf", [128, TC], F32)

    for s in ("LD", "LDS", "LDG", "ST"):
        p.sem(s)

    p.dma("gpsimd", na_sb[:], alog.rearrange("(a p) n -> p a n", p=128)[:, :, :], "LD")
    p.dma("gpsimd", wd_sb[:], wdv.rearrange("(a p) n -> p a n", p=128)[:, :, :], "LD", slow=True)
    p.memset("vector", hst_sb[:], 0.0)
    p.wait("scalar", "LD")
    p.wait("vector", "LD")
    p.act(na_sb.rearrange("p a n -> p (a n)")[:, :], na_sb.rearrange("p a n -> p (a n)")[:, :], AF.Exp)
    p.barrier()
    p.ts(na_sb.rearrange("p a n -> p (a n)")[:, :], na_sb.rearrange("p a n -> p (a n)")[:, :], -1.0, None, OP.mult)
    p.barrier()

    for tc in range(NTC):
        o = tc * TC
        p.dma("gpsimd", brep_sb[:], bc3[0:1, 0:16, o:o + TC].broadcast_to([128, 16, TC]), "LDG")
        p.dma("gpsimd", crep_sb[:], bc3[0:1, 16:32, o:o + TC].broadcast_to([128, 16, TC]), "LDG")
        for et in range(ET2):
            r0 = et * 128
            p.dma("sync", dl_sb[:], delta_s[r0:r0 + 128, o:o + TC], "LDS")
            p.dma("sync", xs_sb[:], xs_s[r0:r0 + 128, o:o + TC], "LDS")
            p.dma("sync", skp_sb[:], skip_s[r0:r0 + 128, o:o + TC], "LDS")
            p.wait("scalar", "LDS")
            p.wait("vector", "LDS")
            p.wait("vector", "LDG")
            p.barrier()
            for n in range(NST):
                p.act(a_sb[:, n, :], dl_sb[:], AF.Exp, scale=na_sb[:, et, n:n + 1])
            p.tt(v_sb[:, 0, :], dl_sb[:], xs_sb[:], OP.mult)
            p.barrier()
            bv, _ = bass.broadcast_tensor_aps(v_sb[:, :, :], brep_sb[:, :, :])
            p.tt(u_sb[:, :, :], bv, brep_sb[:, :, :], OP.mult)
            p.barrier()
            for n in range(NST):
                p.vscan(h_sb[:, n, :], a_sb[:, n, :], u_sb[:, n, :], hst_sb[:, et, n:n + 1])
            p.barrier()
            p.vcopy(hst_sb[:, et, :], h_sb[:, :, TC - 1:TC].rearrange("p n t -> p (n t)"))
            p.tt(u_sb[:, :, :], h_sb[:, :, :], crep_sb[:, :, :], OP.mult)
            p.barrier()
            for half in (8, 4, 2, 1):
                p.tt(u_sb[:, 0:half, :], u_sb[:, 0:half, :], u_sb[:, half:2 * half, :], OP.add)
                p.barrier()
            p.act(silu_sb[:], skp_sb[:], AF.Silu)
            p.stt(yf_sb[:], xs_sb[:], wd_sb[:, et, 0:1], u_sb[:, 0, :], OP.mult, OP.add)
            p.barrier()
            p.tt(yf_sb[:], yf_sb[:], silu_sb[:], OP.mult)
            p.barrier()
            p.dma("sync", yf_out[r0:r0 + 128, o:o + TC], yf_sb[:], "ST")
            p.wait("sync", "ST")
            p.barrier()

    p.wait("sync", "ST")
    return p.build()


def stage2_inmaps(I, s1res):
    alog_full = np.asarray(I["L0_attn_A_log"])
    wd_full = np.asarray(I["L0_attn_W_D"]).reshape(E, 1)
    maps = []
    for c in range(8):
        b, eh = c // 2, c % 2
        lo, hi = eh * (E // 2), (eh + 1) * (E // 2)
        delta = np.concatenate([s1res[2 * b]["delta_out"], s1res[2 * b + 1]["delta_out"]], axis=1)[lo:hi]
        xs = np.concatenate([s1res[2 * b]["xs_out"], s1res[2 * b + 1]["xs_out"]], axis=1)[lo:hi]
        skip = np.concatenate([s1res[2 * b]["skip_out"], s1res[2 * b + 1]["skip_out"]], axis=1)[lo:hi]
        bc = np.concatenate([s1res[2 * b]["bc_out"], s1res[2 * b + 1]["bc_out"]], axis=1).reshape(1, 32, T)
        maps.append(dict(delta_s=np.ascontiguousarray(delta), xs_s=np.ascontiguousarray(xs),
                         skip_s=np.ascontiguousarray(skip), bc3=np.ascontiguousarray(bc),
                         alog=np.ascontiguousarray(alog_full[lo:hi]), wdv=np.ascontiguousarray(wd_full[lo:hi])))
    return maps



def build_stage3():
    p = Prog("s3")
    nc = p.nc
    DT, ET = D // 128, E // 128
    TC = 512
    CO = chunks(0, T2, TC)

    xT = nc.dram_tensor("xT", [D, T2], F32, kind="ExternalInput").ap()
    xnT = nc.dram_tensor("xnT", [D, T2], F32, kind="ExternalInput").ap()
    yfT = nc.dram_tensor("yfT", [E, T2], F32, kind="ExternalInput").ap()
    out_wT = nc.dram_tensor("out_wT", [E, D], F32, kind="ExternalInput").ap()
    enc_w = nc.dram_tensor("enc_w", [D, E], F32, kind="ExternalInput").ap()
    enc_b = nc.dram_tensor("enc_b", [E, 1], F32, kind="ExternalInput").ap()
    s2c_w = nc.dram_tensor("s2c_w", [E, 64], F32, kind="ExternalInput").ap()
    s2c_b = nc.dram_tensor("s2c_b", [64, 1], F32, kind="ExternalInput").ap()
    r_w1 = nc.dram_tensor("r_w1", [64, 64], F32, kind="ExternalInput").ap()
    r_b1 = nc.dram_tensor("r_b1", [64, 1], F32, kind="ExternalInput").ap()
    r_w2 = nc.dram_tensor("r_w2", [64, 8], F32, kind="ExternalInput").ap()
    r_b2r = nc.dram_tensor("r_b2r", [1, 8], F32, kind="ExternalInput").ap()
    e_w1 = nc.dram_tensor("e_w1", [8, 64, 1024], F32, kind="ExternalInput").ap()
    e_b1 = nc.dram_tensor("e_b1", [1024, 8], F32, kind="ExternalInput").ap()
    e_w2 = nc.dram_tensor("e_w2", [8, 1024, 64], F32, kind="ExternalInput").ap()
    e_b2 = nc.dram_tensor("e_b2", [64, 8], F32, kind="ExternalInput").ap()
    c2s_w = nc.dram_tensor("c2s_w", [64, E], F32, kind="ExternalInput").ap()
    c2s_b = nc.dram_tensor("c2s_b", [E, 1], F32, kind="ExternalInput").ap()
    dec_w = nc.dram_tensor("dec_w", [E, D], F32, kind="ExternalInput").ap()
    dec_b = nc.dram_tensor("dec_b", [D, 1], F32, kind="ExternalInput").ap()
    ln_g = nc.dram_tensor("ln_g", [D, 1], F32, kind="ExternalInput").ap()
    ln_b = nc.dram_tensor("ln_b", [D, 1], F32, kind="ExternalInput").ap()
    ninw = nc.dram_tensor("ninw", [D, 1], F32, kind="ExternalInput").ap()
    ident = nc.dram_tensor("ident", [128, 128], F32, kind="ExternalInput").ap()
    ones_col = nc.dram_tensor("ones_col", [128, 1], F32, kind="ExternalInput").ap()
    ones_row = nc.dram_tensor("ones_row", [1, 128], F32, kind="ExternalInput").ap()

    x1_out = nc.dram_tensor("x1_out", [D, T2], F32, kind="ExternalOutput").ap()
    xn1_out = nc.dram_tensor("xn1_out", [D, T2], F32, kind="ExternalOutput").ap()
    hr_out = nc.dram_tensor("hr_out", [D, T2], F32, kind="ExternalOutput").ap()

    encw_sb = [p.sb(f"encw{i}", [128, E], BF16) for i in range(DT)]
    outw_sb = [p.sb(f"outw{i}", [128, D], BF16) for i in range(ET)]
    decwb = [p.sb(f"decwb{i}", [128, 16 * 128], BF16) for i in range(2)]
    s2cw_sb = [p.sb(f"s2cw{i}", [128, 64], BF16) for i in range(ET)]
    rw1_sb = p.sb("rw1", [64, 64], BF16)
    rw2_sb = p.sb("rw2", [64, 8], BF16)
    rb2r_sb = p.sb("rb2r", [128, 8], F32)
    ew1_sb = [p.sb(f"ew1_{e}", [64, 1024], BF16) for e in range(8)]
    ew2_sb = [p.sb(f"ew2_{e}", [128, 8, 64], BF16) for e in range(8)]
    c2sw_sb = p.sb("c2sw", [64, E], BF16)
    eb1_sb = p.sb("eb1", [128, 8, 8], F32)
    eb2_sb = p.sb("eb2", [64, 8], F32)
    encb_sb = p.sb("encb", [128, ET, 1], F32)
    s2cb_sb = p.sb("s2cb", [64, 1], F32)
    rb1_sb = p.sb("rb1", [64, 1], F32)
    c2sb_sb = p.sb("c2sb", [128, ET, 1], F32)
    decb_sb = p.sb("decb", [128, DT, 1], F32)
    lng_sb = p.sb("lng", [128, DT, 1], F32)
    lnb_sb = p.sb("lnb", [128, DT, 1], F32)
    ninw_sb = p.sb("ninw_sb", [128, DT, 1], F32)
    id_sb = p.sb("id_sb", [128, 128], F32)
    ones_sb = p.sb("ones", [128, 1], F32)
    onesr_sb = p.sb("onesr", [1, 128], F32)
    cst = p.sb("cst", [128, 3], F32)

    xnb_sb = p.sb("xnb", [128, DT, TC], BF16)
    xnf_sb = p.sb("xnf", [128, DT, TC], F32)
    xtile_sb = p.sb("xtile", [128, TC], F32)
    yfb_sb = p.sb("yfb", [128, ET, TC], BF16)
    spk_sb = p.sb("spk", [128, ET, TC], BF16)
    cont_sb = p.sb("cont", [64, TC], F32)
    contb_sb = p.sb("contb", [64, TC], BF16)
    rh_sb = p.sb("rh", [64, TC], BF16)
    logit_sb = p.sb("logit", [128, 4, 8], F32)
    prob_sb = p.sb("prob", [128, 4, 8], F32)
    eq_sb = p.sb("eq", [128, 4, 8], F32)
    comb_sb = p.sb("comb", [128, 4, 8], F32)
    rowA = p.sb("rowA", [128, 4, 1], F32)
    rowB = p.sb("rowB", [128, 4, 1], F32)
    rowC = p.sb("rowC", [128, 4, 1], F32)
    ehb_sb = p.sb("ehb", [128, TC], BF16)
    eo_sb = p.sb("eo", [64, TC], F32)
    moe_sb = p.sb("moe", [64, TC], F32)
    moeb_sb = p.sb("moeb", [64, TC], BF16)
    dec_sb = p.sb("dec", [128, DT, TC], F32)
    g0_sb = p.sb("g0", [128, DT, TC], BF16)
    mean_sb = p.sb("meanr", [1, TC], F32)
    rstd_sb = p.sb("rstdr", [1, TC], F32)
    sqt_sb = [p.sb(f"sqt{i}", [128, TC], F32) for i in range(2)]
    trow_sb = sqt_sb[1]
    tmp_sb = p.sb("tmp3", [128, TC], F32)

    PS = [p.ps(f"ps{i}", [128, 512], F32) for i in range(8)]
    for s in ("LD", "LDG", "LDS", "LDW", "ST"):
        p.sem(s)
    p.memset("vector", cst[:, 0:1], 1e-5)
    p.memset("vector", cst[:, 1:2], -1e9)
    p.memset("vector", cst[:, 2:3], 1e-6)

    def col(t3, i, j):
        return t3[:, i, j:j + 1]

    for i in range(DT):
        p.dma("gpsimd", encw_sb[i][:], enc_w[i * 128:(i + 1) * 128, :], "LDG")
    for i in range(ET):
        p.dma("gpsimd", outw_sb[i][:], out_wT[i * 128:(i + 1) * 128, :], "LDG")
        p.dma("gpsimd", s2cw_sb[i][:], s2c_w[i * 128:(i + 1) * 128, :], "LDG")
    p.dma("gpsimd", rw1_sb[:], r_w1[:, :], "LDG")
    p.dma("gpsimd", rw2_sb[:], r_w2[:, :], "LDG")
    for e in range(8):
        p.dma("gpsimd", ew1_sb[e][:], e_w1[e, :, :], "LDG")
        p.dma("gpsimd", ew2_sb[e][:], e_w2[e, :, :].rearrange("(a p) k -> p a k", p=128), "LDG")
    p.dma("gpsimd", c2sw_sb[:], c2s_w[:, :], "LDG")
    p.dma("gpsimd", id_sb[:], ident[:, :], "LDG")
    p.dma("sync", rb2r_sb[:], r_b2r.broadcast_to([128, 8]), "LD")
    p.dma("sync", eb1_sb[:], e_b1.rearrange("(a p) k -> p a k", p=128), "LD")
    p.dma("sync", eb2_sb[:], e_b2[:, :], "LD")
    p.dma("sync", encb_sb[:], enc_b.rearrange("(a p) k -> p a k", p=128), "LD", slow=True)
    p.dma("sync", s2cb_sb[:], s2c_b[:, :], "LD")
    p.dma("sync", rb1_sb[:], r_b1[:, :], "LD")
    p.dma("sync", c2sb_sb[:], c2s_b.rearrange("(a p) k -> p a k", p=128), "LD", slow=True)
    p.dma("sync", decb_sb[:], dec_b.rearrange("(a p) k -> p a k", p=128), "LD", slow=True)
    p.dma("sync", lng_sb[:], ln_g.rearrange("(a p) k -> p a k", p=128), "LD", slow=True)
    p.dma("sync", lnb_sb[:], ln_b.rearrange("(a p) k -> p a k", p=128), "LD", slow=True)
    p.dma("sync", ninw_sb[:], ninw.rearrange("(a p) k -> p a k", p=128), "LD", slow=True)
    p.dma("sync", ones_sb[:], ones_col[:, :], "LD")
    p.dma("sync", onesr_sb[:], ones_row[:, :], "LD")
    for en in ("scalar", "vector", "tensor"):
        p.wait(en, "LD")
        p.wait(en, "LDG")
    p.barrier()

    for (o, w) in CO:
        for i in range(DT):
            p.dma("sync", xnf_sb[:, i, 0:w], xnT[i * 128:(i + 1) * 128, o:o + w], "LDS")
            p.dma("gpsimd", xnb_sb[:, i, 0:w], xnT[i * 128:(i + 1) * 128, o:o + w], "LDG")
        for i in range(ET):
            p.dma("gpsimd", yfb_sb[:, i, 0:w], yfT[i * 128:(i + 1) * 128, o:o + w], "LDG")
        p.wait("tensor", "LDS")
        p.wait("tensor", "LDG")
        p.wait("vector", "LDS")
        p.wait("vector", "LDG")
        p.barrier()
        for et in range(ET):
            for kk in range(DT):
                p.mm(PS[0][:, 0:w], encw_sb[kk][:, et * 128:(et + 1) * 128], xnb_sb[:, kk, 0:w],
                     start=(kk == 0), stop=(kk == DT - 1))
            p.barrier()
            p.act(spk_sb[:, et, 0:w], PS[0][:, 0:w], AF.Sigmoid, bias=col(encb_sb, et, 0))
            p.barrier()
        for et in range(ET):
            p.mm(PS[1][0:64, 0:w], s2cw_sb[et][:], spk_sb[:, et, 0:w], start=(et == 0), stop=(et == ET - 1))
        p.barrier()
        p.ts(cont_sb[:, 0:w], PS[1][0:64, 0:w], s2cb_sb[:, 0:1], None, OP.add)
        p.barrier()
        p.vcopy(contb_sb[:, 0:w], cont_sb[:, 0:w])
        p.barrier()
        p.mm(PS[1][0:64, 0:w], rw1_sb[:], contb_sb[:, 0:w], start=True, stop=True)
        p.barrier()
        p.act(rh_sb[:, 0:w], PS[1][0:64, 0:w], AF.Relu, bias=rb1_sb[:, 0:1])
        p.barrier()
        ntt = w // 128
        for tt in range(ntt):
            p.mm(PS[2][:, tt * 8:tt * 8 + 8], rh_sb[:, tt * 128:(tt + 1) * 128], rw2_sb[:],
                 start=True, stop=True)
        p.barrier()
        p.tt(logit_sb[:, 0:ntt, :], PS[2][:, 0:ntt * 8].rearrange("p (a b) -> p a b", b=8), rb2r_sb[:].rearrange("p (a b) -> p a b", a=1).broadcast_to([128, ntt, 8]), OP.add)
        p.barrier()
        p.vreduce(rowA[:, 0:ntt, :], logit_sb[:, 0:ntt, :], OP.max)
        p.barrier()
        p.ts(rowB[:, 0:ntt, :], rowA[:, 0:ntt, :], -1.0, None, OP.mult)
        p.barrier()
        for tt in range(ntt):
            p.act(prob_sb[:, tt, :], logit_sb[:, tt, :], AF.Exp, bias=col(rowB, tt, 0),
                  accum_out=rowA[:, tt, :])
        p.barrier()
        p.vrecip(rowB[:, 0:ntt, :], rowA[:, 0:ntt, :])
        p.barrier()
        for tt in range(ntt):
            p.ts(prob_sb[:, tt, :], prob_sb[:, tt, :], col(rowB, tt, 0), None, OP.mult)
        p.barrier()
        p.vreduce(rowA[:, 0:ntt, :], prob_sb[:, 0:ntt, :], OP.max)
        p.barrier()
        for tt in range(ntt):
            p.ts(eq_sb[:, tt, :], prob_sb[:, tt, :], col(rowA, tt, 0), None, OP.is_equal)
        p.barrier()
        for tt in range(ntt):
            p.stt(comb_sb[:, tt, :], eq_sb[:, tt, :], cst[:, 1:2], prob_sb[:, tt, :], OP.mult, OP.add)
        p.barrier()
        p.vreduce(rowB[:, 0:ntt, :], comb_sb[:, 0:ntt, :], OP.max)
        p.barrier()
        p.tt(rowC[:, 0:ntt, :], rowA[:, 0:ntt, :], rowB[:, 0:ntt, :], OP.add)
        p.barrier()
        p.vrecip(rowC[:, 0:ntt, :], rowC[:, 0:ntt, :])
        p.barrier()
        for tt in range(ntt):
            p.ts(prob_sb[:, tt, :], comb_sb[:, tt, :], col(rowB, tt, 0), None, OP.is_equal)
        p.barrier()
        p.tt(rowA[:, 0:ntt, :], rowA[:, 0:ntt, :], rowC[:, 0:ntt, :], OP.mult)
        p.tt(rowB[:, 0:ntt, :], rowB[:, 0:ntt, :], rowC[:, 0:ntt, :], OP.mult)
        p.barrier()
        for tt in range(ntt):
            p.ts(prob_sb[:, tt, :], prob_sb[:, tt, :], col(rowB, tt, 0), None, OP.mult)
        p.barrier()
        for tt in range(ntt):
            p.stt(comb_sb[:, tt, :], eq_sb[:, tt, :], col(rowA, tt, 0), prob_sb[:, tt, :], OP.mult, OP.add)
        p.barrier()
        p.memset("vector", moe_sb[:, 0:w], 0.0)
        p.barrier()
        for e in range(8):
            for mt in range(8):
                p.mm(PS[3][:, 0:w], ew1_sb[e][:, mt * 128:(mt + 1) * 128], contb_sb[:, 0:w],
                     start=True, stop=True)
                p.barrier()
                p.act(ehb_sb[:, 0:w], PS[3][:, 0:w], AF.Relu, bias=col(eb1_sb, mt, e))
                p.barrier()
                p.mm(PS[4][0:64, 0:w], ew2_sb[e][:, mt, :], ehb_sb[:, 0:w],
                     start=(mt == 0), stop=(mt == 7))
                p.barrier()
            p.ts(eo_sb[:, 0:w], PS[4][0:64, 0:w], eb2_sb[:, e:e + 1], None, OP.add)
            p.barrier()
            for tt in range(ntt):
                lhs_b = comb_sb[:, tt, e:e + 1].broadcast_to([128, 64])
                p.mm(PS[5][0:64, tt * 128:tt * 128 + 128], lhs_b, id_sb[:], start=True, stop=True)
            p.barrier()
            p.tt(eo_sb[:, 0:w], eo_sb[:, 0:w], PS[5][0:64, 0:w], OP.mult)
            p.barrier()
            p.tt(moe_sb[:, 0:w], moe_sb[:, 0:w], eo_sb[:, 0:w], OP.add)
            p.barrier()
        p.vcopy(moeb_sb[:, 0:w], moe_sb[:, 0:w])
        p.barrier()
        for et in range(ET):
            p.mm(PS[0][:, 0:w], c2sw_sb[:, et * 128:(et + 1) * 128], moeb_sb[:, 0:w], start=True, stop=True)
            p.barrier()
            p.act(spk_sb[:, et, 0:w], PS[0][:, 0:w], AF.Sigmoid, bias=col(c2sb_sb, et, 0))
            p.barrier()
        for kk in range(ET):
            p.dma("gpsimd", decwb[0][:, kk * 128:(kk + 1) * 128],
                  dec_w[kk * 128:(kk + 1) * 128, 0:128], "LDW")
        for dt in range(DT):
            bufd = dt % 2
            p.wait("tensor", "LDW", p.tgt("LDW"))
            p.barrier()
            if dt + 1 < DT:
                for kk in range(ET):
                    p.dma("gpsimd", decwb[bufd ^ 1][:, kk * 128:(kk + 1) * 128],
                          dec_w[kk * 128:(kk + 1) * 128, (dt + 1) * 128:(dt + 2) * 128], "LDW")
            for kk in range(ET):
                p.mm(PS[0][:, 0:w], decwb[bufd][:, kk * 128:(kk + 1) * 128], spk_sb[:, kk, 0:w],
                     start=(kk == 0), stop=(kk == ET - 1))
            p.barrier()
            p.act(dec_sb[:, dt, 0:w], PS[0][:, 0:w], AF.Sigmoid, bias=col(decb_sb, dt, 0))
            p.barrier()
        for i in range(DT):
            p.act(sqt_sb[i % 2][:, 0:w], dec_sb[:, i, 0:w], AF.Square)
            p.barrier()
            p.mm(PS[6][0:1, 0:w], ones_sb[:], dec_sb[:, i, 0:w], start=(i == 0), stop=(i == DT - 1),
                 skip_group_check=True)
            p.mm(PS[7][0:1, 0:w], ones_sb[:], sqt_sb[i % 2][:, 0:w], start=(i == 0), stop=(i == DT - 1),
                 skip_group_check=True)
        p.barrier()
        p.ts(mean_sb[0:1, 0:w], PS[6][0:1, 0:w], 1.0 / D)
        p.ts(rstd_sb[0:1, 0:w], PS[7][0:1, 0:w], 1.0 / D)
        p.barrier()
        p.act(trow_sb[0:1, 0:w], mean_sb[0:1, 0:w], AF.Square)
        p.barrier()
        p.tt(rstd_sb[0:1, 0:w], rstd_sb[0:1, 0:w], trow_sb[0:1, 0:w], OP.subtract)
        p.barrier()
        p.act(trow_sb[0:1, 0:w], rstd_sb[0:1, 0:w], AF.Sqrt, bias=cst[0:1, 0:1])
        p.barrier()
        p.vrecip(rstd_sb[0:1, 0:w], trow_sb[0:1, 0:w])
        p.barrier()
        p.mm(PS[6][:, 0:w], onesr_sb[:], mean_sb[0:1, 0:w], start=True, stop=True)
        p.mm(PS[7][:, 0:w], onesr_sb[:], rstd_sb[0:1, 0:w], start=True, stop=True)
        p.barrier()
        for i in range(DT):
            p.tt(tmp_sb[:, 0:w], dec_sb[:, i, 0:w], PS[6][:, 0:w], OP.subtract)
            p.barrier()
            p.tt(tmp_sb[:, 0:w], tmp_sb[:, 0:w], PS[7][:, 0:w], OP.mult)
            p.barrier()
            p.ts(tmp_sb[:, 0:w], tmp_sb[:, 0:w], col(lng_sb, i, 0), col(lnb_sb, i, 0), OP.mult, OP.add)
            p.barrier()
            p.act(g0_sb[:, i, 0:w], tmp_sb[:, 0:w], AF.Silu)
            p.barrier()
        for dt in range(DT):
            for kk in range(ET):
                p.mm(PS[0][:, 0:w], outw_sb[kk][:, dt * 128:(dt + 1) * 128], yfb_sb[:, kk, 0:w],
                     start=(kk == 0), stop=(kk == ET - 1))
            p.barrier()
            p.tt(dec_sb[:, dt, 0:w], xnf_sb[:, dt, 0:w], PS[0][:, 0:w], OP.add)
            p.barrier()
        for dt in range(DT):
            p.dma("sync", xtile_sb[:, 0:w], xT[dt * 128:(dt + 1) * 128, o:o + w], "LDS")
            p.wait("vector", "LDS")
            p.barrier()
            p.tt(tmp_sb[:, 0:w], dec_sb[:, dt, 0:w], g0_sb[:, dt, 0:w], OP.mult)
            p.barrier()
            p.tt(tmp_sb[:, 0:w], xtile_sb[:, 0:w], tmp_sb[:, 0:w], OP.add)
            p.barrier()
            p.dma("sync", x1_out[dt * 128:(dt + 1) * 128, o:o + w], tmp_sb[:, 0:w], "ST")
            p.wait("vector", "ST")
            p.barrier()

    for (o, w) in CO:
        for i in range(DT):
            p.dma("sync", dec_sb[:, i, 0:w], x1_out[i * 128:(i + 1) * 128, o:o + w], "LDS")
        p.wait("scalar", "LDS")
        p.wait("vector", "LDS")
        p.wait("tensor", "LDS")
        p.barrier()
        for i in range(DT):
            p.act(sqt_sb[i % 2][:, 0:w], dec_sb[:, i, 0:w], AF.Square)
            p.barrier()
            p.mm(PS[6][0:1, 0:w], ones_sb[:], dec_sb[:, i, 0:w], start=(i == 0), stop=(i == DT - 1),
                 skip_group_check=True)
            p.mm(PS[7][0:1, 0:w], ones_sb[:], sqt_sb[i % 2][:, 0:w], start=(i == 0), stop=(i == DT - 1),
                 skip_group_check=True)
        p.barrier()
        p.ts(mean_sb[0:1, 0:w], PS[6][0:1, 0:w], 1.0 / D)
        p.ts(rstd_sb[0:1, 0:w], PS[7][0:1, 0:w], 1.0 / D)
        p.barrier()
        p.act(trow_sb[0:1, 0:w], mean_sb[0:1, 0:w], AF.Square)
        p.barrier()
        p.tt(rstd_sb[0:1, 0:w], rstd_sb[0:1, 0:w], trow_sb[0:1, 0:w], OP.subtract)
        p.barrier()
        p.act(trow_sb[0:1, 0:w], rstd_sb[0:1, 0:w], AF.Sqrt, bias=cst[0:1, 0:1])
        p.barrier()
        p.vrecip(rstd_sb[0:1, 0:w], trow_sb[0:1, 0:w])
        p.barrier()
        p.mm(PS[6][:, 0:w], onesr_sb[:], mean_sb[0:1, 0:w], start=True, stop=True)
        p.mm(PS[7][:, 0:w], onesr_sb[:], rstd_sb[0:1, 0:w], start=True, stop=True)
        p.barrier()
        for i in range(DT):
            p.tt(dec_sb[:, i, 0:w], dec_sb[:, i, 0:w], PS[6][:, 0:w], OP.subtract)
            p.barrier()
            p.tt(dec_sb[:, i, 0:w], dec_sb[:, i, 0:w], PS[7][:, 0:w], OP.mult)
            p.barrier()
            p.dma("sync", xn1_out[i * 128:(i + 1) * 128, o:o + w], dec_sb[:, i, 0:w], "ST")
        for i in range(DT):
            p.act(sqt_sb[i % 2][:, 0:w], dec_sb[:, i, 0:w], AF.Square)
            p.barrier()
            p.mm(PS[7][0:1, 0:w], ones_sb[:], sqt_sb[i % 2][:, 0:w], start=(i == 0), stop=(i == DT - 1),
                 skip_group_check=True)
        p.barrier()
        p.ts(rstd_sb[0:1, 0:w], PS[7][0:1, 0:w], 1.0 / D)
        p.barrier()
        p.act(trow_sb[0:1, 0:w], rstd_sb[0:1, 0:w], AF.Sqrt, bias=cst[0:1, 2:3])
        p.barrier()
        p.vrecip(rstd_sb[0:1, 0:w], trow_sb[0:1, 0:w])
        p.barrier()
        p.mm(PS[7][:, 0:w], onesr_sb[:], rstd_sb[0:1, 0:w], start=True, stop=True)
        p.barrier()
        for i in range(DT):
            p.tt(tmp_sb[:, 0:w], dec_sb[:, i, 0:w], PS[7][:, 0:w], OP.mult)
            p.barrier()
            p.ts(tmp_sb[:, 0:w], tmp_sb[:, 0:w], col(ninw_sb, i, 0), None, OP.mult)
            p.barrier()
            p.dma("sync", hr_out[i * 128:(i + 1) * 128, o:o + w], tmp_sb[:, 0:w], "ST")
            p.wait("vector", "ST")
            p.barrier()

    p.wait("sync", "ST")
    return p.build()


def stage3_inmaps(I, s1res, s2res):
    shared = dict(
        out_wT=_t(I["L0_attn_out_w"]),
        enc_w=np.ascontiguousarray(I["L0_ffn_enc_w"]),
        enc_b=np.asarray(I["L0_ffn_enc_b"]).reshape(E, 1),
        s2c_w=np.ascontiguousarray(I["L0_ffn_s2c_w"]),
        s2c_b=np.asarray(I["L0_ffn_s2c_b"]).reshape(64, 1),
        r_w1=np.ascontiguousarray(I["L0_ffn_r_w1"]),
        r_b1=np.asarray(I["L0_ffn_r_b1"]).reshape(64, 1),
        r_w2=np.ascontiguousarray(I["L0_ffn_r_w2"]),
        r_b2r=np.asarray(I["L0_ffn_r_b2"]).reshape(1, 8),
        e_w1=np.ascontiguousarray(I["L0_ffn_e_w1"]),
        e_b1=_t(I["L0_ffn_e_b1"]),
        e_w2=np.ascontiguousarray(I["L0_ffn_e_w2"]),
        e_b2=_t(I["L0_ffn_e_b2"]),
        c2s_w=np.ascontiguousarray(I["L0_ffn_c2s_w"]),
        c2s_b=np.asarray(I["L0_ffn_c2s_b"]).reshape(E, 1),
        dec_w=np.ascontiguousarray(I["L0_ffn_dec_w"]),
        dec_b=np.asarray(I["L0_ffn_dec_b"]).reshape(D, 1),
        ln_g=np.asarray(I["L0_ffn_ln_g"]).reshape(D, 1),
        ln_b=np.asarray(I["L0_ffn_ln_b"]).reshape(D, 1),
        ninw=np.asarray(I["L1_attn_norm_in_w"]).reshape(D, 1),
        ident=np.eye(128, dtype=np.float32),
        ones_col=np.ones((128, 1), np.float32),
        ones_row=np.ones((1, 128), np.float32),
    )
    x = np.asarray(I["L0_attn_resid"])
    maps = []
    for c in range(8):
        b, j = c // 2, c % 2
        sl = slice(j * T2, (j + 1) * T2)
        yf = np.concatenate([s2res[2 * b]["yf_out"], s2res[2 * b + 1]["yf_out"]], axis=0)[:, sl]
        m = dict(shared)
        m["xT"] = _t(x[b, sl])
        m["xnT"] = np.ascontiguousarray(s1res[c]["xn_out"])
        m["yfT"] = np.ascontiguousarray(yf)
        maps.append(m)
    return maps



def build_stage4(cut=None, dbg=False):
    p = Prog("s4")
    nc = p.nc
    TG = 512
    NG = T // TG
    CC = 32
    NCH = TG // CC
    T4H = TG + 4

    hr_s = nc.dram_tensor("hr_s", [D, T + 4], F32, kind="ExternalInput").ap()
    cw1 = nc.dram_tensor("cw1", [D, 4], F32, kind="ExternalInput").ap()
    cb1 = nc.dram_tensor("cb1", [D, 1], F32, kind="ExternalInput").ap()
    wq = nc.dram_tensor("wq", [D, 256], F32, kind="ExternalInput").ap()
    wk = nc.dram_tensor("wk", [D, 256], F32, kind="ExternalInput").ap()
    wv = nc.dram_tensor("wv", [D, 256], F32, kind="ExternalInput").ap()
    wsp = nc.dram_tensor("wsp", [D, 256], F32, kind="ExternalInput").ap()
    wau = nc.dram_tensor("wau", [D, 64], F32, kind="ExternalInput").ap()
    bau = nc.dram_tensor("bau", [64, 1], F32, kind="ExternalInput").ap()
    wad = nc.dram_tensor("wad", [64, 256], F32, kind="ExternalInput").ap()
    bad4 = nc.dram_tensor("bad4", [64, 4], F32, kind="ExternalInput").ap()
    wbe = nc.dram_tensor("wbe", [D, 4], F32, kind="ExternalInput").ap()
    bbe = nc.dram_tensor("bbe", [4, 1], F32, kind="ExternalInput").ap()
    asp4 = nc.dram_tensor("asp4", [64, 4], F32, kind="ExternalInput").ap()
    bspsel = nc.dram_tensor("bspsel", [64, 4, 4], F32, kind="ExternalInput").ap()
    sel4 = nc.dram_tensor("sel4", [64, 4, 4], F32, kind="ExternalInput").ap()
    e4h = nc.dram_tensor("e4h", [4, 4, 64], F32, kind="ExternalInput").ap()
    ident = nc.dram_tensor("ident", [128, 128], F32, kind="ExternalInput").ap()
    trilS = nc.dram_tensor("trilS", [CC, CC], F32, kind="ExternalInput").ap()
    trilI = nc.dram_tensor("trilI", [CC, CC], F32, kind="ExternalInput").ap()
    trilL = nc.dram_tensor("trilL", [CC, CC], F32, kind="ExternalInput").ap()
    y_out = nc.dram_tensor("y_out", [T, 256], F32, kind="ExternalOutput").ap()
    if dbg:
        dbg_out = {nm: nc.dram_tensor("dbg_" + nm, [64, 4 * 512], F32, kind="ExternalOutput").ap()
                   for nm in ("qn", "kn", "v", "dr", "spike", "lam", "gg", "ab")}
        dbg_rows = nc.dram_tensor("dbg_rows", [4, 3 * 512], F32, kind="ExternalOutput").ap()

    DT = D // 128

    wq_sb = [p.sb(f"wq{i}", [128, 256], BF16) for i in range(DT)]
    wk_sb = [p.sb(f"wk{i}", [128, 256], BF16) for i in range(DT)]
    wv_sb = [p.sb(f"wv{i}", [128, 256], BF16) for i in range(DT)]
    wsp_sb = [p.sb(f"wsp{i}", [128, 256], BF16) for i in range(DT)]
    wau_sb = [p.sb(f"wau{i}", [128, 64], BF16) for i in range(DT)]
    wad_sb = p.sb("wad_sb", [64, 256], BF16)
    wbe_sb = [p.sb(f"wbe{i}", [128, 4], BF16) for i in range(DT)]
    cw_sb = p.sb("cw1_sb", [128, DT, 4], F32)
    cb_sb = p.sb("cb1_sb", [128, DT, 1], F32)
    bau_sb = p.sb("bau_sb", [64, 1], F32)
    bad_sb = p.sb("bad_sb", [64, 4], F32)
    bbe_sb = p.sb("bbe_sb", [4, 1], F32)
    asp_sb = p.sb("asp_sb", [64, 4], F32)
    bsps_sb = p.sb("bsps_sb", [64, 4, 4], F32)
    sel_sb = p.sb("sel_sb", [64, 4, 4], F32)
    e4_sb = p.sb("e4_sb", [4, 4, 64], F32)
    id_sb = p.sb("id4_sb", [128, 128], F32)
    mS_sb = p.sb("mS_sb", [CC, CC], F32)
    mI_sb = p.sb("mI_sb", [CC, CC], F32)
    mL_sb = p.sb("mL_sb", [CC, CC], F32)
    cst = p.sb("cst4", [128, 2], F32)

    hr_sb = p.sb("hr4", [128, DT, T4H], F32)
    hc_sb = p.sb("hc4", [128, DT, TG], BF16)
    cvt_sb = p.sb("cvt", [128, TG], F32)
    qn_sb = p.sb("qn", [64, 4, TG], F32)
    kn_sb = p.sb("kn", [64, 4, TG], F32)
    v_sb = p.sb("v4", [64, 4, TG], F32)
    dr_sb = p.sb("dr", [64, 4, TG], F32)
    ab_sb = p.sb("ab", [64, 4, TG], F32)
    aub_sb = p.sb("aub", [64, TG], BF16)
    spike_sb = p.sb("spike", [64, 4, TG], F32)
    lam_sb = p.sb("lam", [64, 4, TG], F32)
    gg_sb = p.sb("gg", [64, 4, TG], F32)
    fq_sb = p.sb("fq", [64, 4, TG], F32)
    fk_sb = p.sb("fk", [64, 4, TG], F32)
    eg_sb = p.sb("eg", [64, 4, TG], F32)
    ee_sb = p.sb("ee", [64, 4, TG], F32)
    kq_sb = p.sb("kq", [64, 4, TG], F32)
    qq_sb = p.sb("qq", [64, 4, TG], F32)
    kkb_sb = p.sb("kkb", [64, 4, TG], F32)
    kbar_sb = p.sb("kbar", [64, 4, TG], F32)
    qbar_sb = p.sb("qbar", [64, 4, TG], F32)
    khat_sb = p.sb("khat", [64, 4, TG], F32)
    mem_sb = p.sb("mem", [64, 4], F32)
    bb_row = p.sb("bb_row", [4, TG], F32)
    act_row = p.sb("act_row", [4, TG], F32)
    beta_row = p.sb("beta_row", [4, TG], F32)
    tr4 = p.sb("tr4", [4, TG], F32)
    nrm_row = p.sb("nrm_row", [4, TG], F32)
    btT_sb = p.sb("btT", [CC, 4], F32)
    S_sb = [p.sb(f"S4_{h}", [64, 64], F32) for h in range(4)]
    As_sb = p.sb("As", [CC, CC], F32)
    Am_sb = p.sb("Am", [CC, CC], F32)
    A2_sb = p.sb("A2", [CC, CC], F32)
    A2T_sb = p.sb("A2T", [CC, CC], F32)
    A4_sb = p.sb("A4", [CC, CC], F32)
    A4T_sb = p.sb("A4T", [CC, CC], F32)
    A8_sb = p.sb("A8", [CC, CC], F32)
    A8T_sb = p.sb("A8T", [CC, CC], F32)
    A16_sb = p.sb("A16", [CC, CC], F32)
    Bs_sb = p.sb("Bs", [CC, CC], F32)
    vt_sb = p.sb("vt", [CC, 64], F32)
    kht_sb = p.sb("kht", [CC, 64], F32)
    X_sb = p.sb("X4", [CC, 64], F32)
    ybuf_sb = p.sb("ybuf", [CC, 4, 64], F32)

    PS = [p.ps(f"ps{i}", [128, 512], F32) for i in range(8)]
    for s in ("LD", "LDG", "LDS", "ST"):
        p.sem(s)
    p.memset("vector", cst[:, 0:1], 1.0)
    p.memset("vector", cst[:, 1:2], 1e-6)
    p.memset("vector", mem_sb[:], 0.0)
    for h in range(4):
        p.memset("vector", S_sb[h][:], 0.0)

    def col(t3, i, j):
        return t3[:, i, j:j + 1]

    for i in range(DT):
        r = slice(i * 128, (i + 1) * 128)
        p.dma("gpsimd", wq_sb[i][:], wq[r, :], "LDG")
        p.dma("gpsimd", wk_sb[i][:], wk[r, :], "LDG")
        p.dma("gpsimd", wv_sb[i][:], wv[r, :], "LDG")
        p.dma("gpsimd", wsp_sb[i][:], wsp[r, :], "LDG")
        p.dma("gpsimd", wau_sb[i][:], wau[r, :], "LDG")
        p.dma("gpsimd", wbe_sb[i][:], wbe[r, :], "LDG")
    p.dma("gpsimd", wad_sb[:], wad[:, :], "LDG")
    p.dma("sync", cw_sb[:], cw1.rearrange("(a p) k -> p a k", p=128), "LD")
    p.dma("sync", cb_sb[:], cb1.rearrange("(a p) k -> p a k", p=128), "LD", slow=True)
    p.dma("sync", bau_sb[:], bau[:, :], "LD")
    p.dma("sync", bad_sb[:], bad4[:, :], "LD")
    p.dma("sync", bbe_sb[:], bbe[:, :], "LD")
    p.dma("sync", asp_sb[:], asp4[:, :], "LD")
    p.dma("sync", bsps_sb[:], bspsel[:, :, :], "LD")
    p.dma("sync", sel_sb[:], sel4[:, :, :], "LD")
    p.dma("sync", e4_sb[:], e4h[:, :, :], "LD")
    p.dma("sync", id_sb[:], ident[:, :], "LD")
    p.dma("sync", mS_sb[:], trilS[:, :], "LD")
    p.dma("sync", mI_sb[:], trilI[:, :], "LD")
    p.dma("sync", mL_sb[:], trilL[:, :], "LD")
    for en in ("scalar", "vector", "tensor"):
        p.wait(en, "LD")
        p.wait(en, "LDG")
    p.barrier()
    p.ts(asp_sb[:, :], asp_sb[:, :], 2.0, None, OP.mult)
    p.ts(bsps_sb.rearrange("p a k -> p (a k)")[:, :], bsps_sb.rearrange("p a k -> p (a k)")[:, :],
         2.0, None, OP.mult)
    p.barrier()

    NSC = TG // 512

    for g in range(NG):
        o = g * TG
        for i in range(DT):
            p.dma("sync", hr_sb[:, i, :], hr_s[i * 128:(i + 1) * 128, o:o + T4H], "LDS")
        p.wait("vector", "LDS")
        p.wait("scalar", "LDS")
        p.barrier()
        for i in range(DT):
            p.ts(cvt_sb[:], hr_sb[:, i, 4:4 + TG], col(cw_sb, i, 3), col(cb_sb, i, 0), OP.mult, OP.add)
            for kk in range(3):
                p.stt(cvt_sb[:], hr_sb[:, i, 1 + kk:1 + kk + TG], col(cw_sb, i, kk), cvt_sb[:], OP.mult, OP.add)
            p.act(hc_sb[:, i, :], cvt_sb[:], AF.Silu)
        p.barrier()

        def proj(w_tiles, dst):
            for f in range(2):
                for kk in range(DT):
                    p.mm(PS[0][:, 0:512], w_tiles[kk][:, f * 128:(f + 1) * 128],
                         hc_sb[:, kk, 0:512], start=(kk == 0), stop=(kk == DT - 1))
                p.vcopy(dst[:, 2 * f, :], PS[0][0:64, 0:512])
                p.vcopy(dst[:, 2 * f + 1, :], PS[0][64:128, 0:512])
        proj(wq_sb, qn_sb)
        if cut == 'proj1':
            p.wait('sync', 'ST')
            return p.build()
        proj(wk_sb, kn_sb)
        proj(wv_sb, v_sb)
        proj(wsp_sb, dr_sb)
        for kk in range(DT):
            p.mm(PS[1][0:64, 0:512], wau_sb[kk][:], hc_sb[:, kk, 0:512], start=(kk == 0), stop=(kk == DT - 1))
        p.act(aub_sb[:, 0:512], PS[1][0:64, 0:512], AF.Silu, bias=bau_sb[:, 0:1])
        for f in range(2):
            p.mm(PS[1][:, 0:512], wad_sb[:, f * 128:(f + 1) * 128], aub_sb[:, 0:512], start=True, stop=True)
            p.ts(ab_sb[:, 2 * f, :], PS[1][0:64, 0:512], bad_sb[:, 2 * f:2 * f + 1], None, OP.add)
            p.ts(ab_sb[:, 2 * f + 1, :], PS[1][64:128, 0:512], bad_sb[:, 2 * f + 1:2 * f + 2], None, OP.add)
        for kk in range(DT):
            p.mm(PS[1][0:4, 0:512], wbe_sb[kk][:], hc_sb[:, kk, 0:512], start=(kk == 0), stop=(kk == DT - 1))
        p.ts(bb_row[:, :], PS[1][0:4, 0:512], bbe_sb[:, 0:1], None, OP.add)
        p.barrier()
        for arr in (qn_sb, kn_sb):
            for h in range(4):
                p.act(fq_sb[:, h, :], arr[:, h, :], AF.Square)
            p.barrier()
            for h in range(4):
                p.mm(PS[2][0:4, 0:512], sel_sb[:, h, :], fq_sb[:, h, 0:512], start=(h == 0), stop=(h == 3))
            p.barrier()
            p.act(nrm_row[:, :], PS[2][0:4, 0:512], AF.Sqrt)
            p.ts(nrm_row[:, :], nrm_row[:, :], 1e-6, None, OP.add, OP.add) if False else None
            p.ts(nrm_row[:, :], nrm_row[:, :], 1e-6, None, OP.add)
            p.vrecip(nrm_row[:, :], nrm_row[:, :])
            p.barrier()
            for h in range(4):
                p.mm(PS[2][0:64, 0:512], e4_sb[:, h, :], nrm_row[:, 0:512], start=True, stop=True)
                p.barrier()
                p.tt(arr[:, h, :], arr[:, h, :], PS[2][0:64, 0:512], OP.mult)
            p.barrier()
        if cut == 'norm':
            p.wait('sync', 'ST')
            return p.build()
        for t in range(TG):
            p.stt(cvt_sb[0:64, 0:4], mem_sb[:], 0.9, dr_sb[:, :, t], OP.mult, OP.add)
            p.ts(spike_sb[:, :, t], cvt_sb[0:64, 0:4], 0.5, 0.5, OP.is_gt, OP.mult)
            p.tt(mem_sb[:], cvt_sb[0:64, 0:4], spike_sb[:, :, t], OP.subtract)
        p.barrier()
        for h in range(4):
            p.stt(lam_sb[:, h, :], spike_sb[:, h, :], asp_sb[:, h:h + 1], ab_sb[:, h, :], OP.mult, OP.add)
        p.barrier()
        p.act(lam_sb.rearrange("p a t -> p (a t)")[:, :], lam_sb.rearrange("p a t -> p (a t)")[:, :],
              AF.Exp, scale=-1.0)
        p.act(lam_sb.rearrange("p a t -> p (a t)")[:, :], lam_sb.rearrange("p a t -> p (a t)")[:, :],
              AF.Ln, bias=cst[0:64, 0:1])
        p.barrier()
        for h in range(4):
            p.mm(PS[2][0:4, 0:512], sel_sb[:, h, :], spike_sb[:, h, 0:512], start=(h == 0), stop=(h == 3))
        p.barrier()
        p.ts(act_row[:, :], PS[2][0:4, 0:512], 0.0, 1.0, OP.is_gt, OP.mult)
        p.barrier()
        for h in range(4):
            p.mm(PS[3][0:4, 0:512], bsps_sb[:, h, :], spike_sb[:, h, 0:512], start=(h == 0), stop=(h == 3))
        p.barrier()
        p.vcopy(tr4[:, :], PS[3][0:4, 0:512])
        p.tt(tr4[:, :], tr4[:, :], bb_row[:, :], OP.add)
        p.barrier()
        p.act(tr4[:, :], tr4[:, :], AF.Sigmoid)
        p.barrier()
        p.tt(beta_row[:, :], tr4[:, :], act_row[:, :], OP.mult)
        p.barrier()
        for h in range(4):
            p.mm(PS[2][0:64, 0:512], e4_sb[:, h, :], act_row[:, 0:512], start=True, stop=True)
            p.barrier()
            p.tt(lam_sb[:, h, :], lam_sb[:, h, :], PS[2][0:64, 0:512], OP.mult)
        p.barrier()
        if dbg and g == 0:
            for nm, arr in (("qn", qn_sb), ("kn", kn_sb), ("v", v_sb), ("dr", dr_sb),
                            ("spike", spike_sb), ("lam", lam_sb), ("ab", ab_sb)):
                p.dma("sync", dbg_out[nm][:, :], arr.rearrange("p a t -> p (a t)")[:, :], "ST")
            p.dma("sync", dbg_rows[:, 0:512], beta_row[:, :], "ST")
            p.dma("sync", dbg_rows[:, 512:1024], act_row[:, :], "ST")
            p.dma("sync", dbg_rows[:, 1024:1536], bb_row[:, :], "ST")
            p.wait("sync", "ST")
        p.vcopy(gg_sb.rearrange("p a t -> p (a t)")[:, :], lam_sb.rearrange("p a t -> p (a t)")[:, :])
        gv = gg_sb.rearrange("p a (c u) -> p (a c) u", u=CC)
        for d in (1, 2, 4, 8, 16):
            p.tt(gv[:, :, d:CC], gv[:, :, d:CC], gv[:, :, 0:CC - d], OP.add)
        p.barrier()
        if dbg and g == 0:
            p.dma("sync", dbg_out["gg"][:, :], gg_sb.rearrange("p a t -> p (a t)")[:, :], "ST")
            p.wait("sync", "ST")
        gvm = gg_sb.rearrange("p a (c u) -> p a c u", u=CC)
        mid = gvm[:, :, :, CC // 2 - 1:CC // 2]
        bmid = bass.broadcast_tensor_aps(gvm[:, :, :, :], mid)[1]
        p.tt(fq_sb.rearrange("p a (c u) -> p a c u", u=CC)[:, :, :, :], gvm[:, :, :, :], bmid, OP.subtract)
        p.barrier()
        p.act(fk_sb.rearrange("p a t -> p (a t)")[:, :], fq_sb.rearrange("p a t -> p (a t)")[:, :], AF.Exp)
        p.act(fq_sb.rearrange("p a t -> p (a t)")[:, :], fq_sb.rearrange("p a t -> p (a t)")[:, :],
              AF.Exp, scale=-1.0)
        p.act(eg_sb.rearrange("p a t -> p (a t)")[:, :], gg_sb.rearrange("p a t -> p (a t)")[:, :],
              AF.Exp, scale=-1.0)
        p.barrier()
        gend = gvm[:, :, :, CC - 1:CC]
        bend = bass.broadcast_tensor_aps(gvm[:, :, :, :], gend)[1]
        p.tt(ee_sb.rearrange("p a (c u) -> p a c u", u=CC)[:, :, :, :], gvm[:, :, :, :], bend, OP.subtract)
        p.barrier()
        p.act(ee_sb.rearrange("p a t -> p (a t)")[:, :], ee_sb.rearrange("p a t -> p (a t)")[:, :], AF.Exp)
        p.barrier()
        for a, bsrc, fac in ((kq_sb, kn_sb, fq_sb), (qq_sb, qn_sb, fq_sb), (kbar_sb, kn_sb, eg_sb),
                             (qbar_sb, qn_sb, eg_sb), (khat_sb, kn_sb, ee_sb), (kkb_sb, kn_sb, fk_sb)):
            p.tt(a.rearrange("p a t -> p (a t)")[:, :], bsrc.rearrange("p a t -> p (a t)")[:, :],
                 fac.rearrange("p a t -> p (a t)")[:, :], OP.mult)
        p.barrier()
        for h in range(4):
            p.mm(PS[2][0:64, 0:512], e4_sb[:, h, :], beta_row[:, 0:512], start=True, stop=True)
            p.barrier()
            p.tt(kkb_sb[:, h, :], kkb_sb[:, h, :], PS[2][0:64, 0:512], OP.mult)
        p.barrier()

        if cut == 'prechunk':
            p.wait('sync', 'ST')
            return p.build()
        for cc in range(NCH):
            c0 = cc * CC
            p.trans(PS[6][0:CC, 0:4], beta_row[:, c0:c0 + CC], id_sb[0:4, 0:4])
            p.barrier()
            p.vcopy(btT_sb[:], PS[6][0:CC, 0:4])
            for h in range(4):
                def sl(arr):
                    return arr[:, h, c0:c0 + CC]
                p.mm(PS[0][0:CC, 0:CC], sl(kkb_sb), sl(kq_sb), start=True, stop=True)
                p.mm(PS[1][0:CC, 0:CC], sl(kq_sb), sl(kkb_sb), start=True, stop=True)
                p.mm(PS[2][0:CC, 0:CC], sl(kkb_sb), sl(qq_sb), start=True, stop=True)
                p.tt(As_sb[:], PS[0][0:CC, 0:CC], mS_sb[:], OP.mult)
                p.tt(Am_sb[:], PS[1][0:CC, 0:CC], mL_sb[:], OP.mult)
                p.tt(Bs_sb[:], PS[2][0:CC, 0:CC], mI_sb[:], OP.mult)
                p.mm(PS[0][0:CC, 0:CC], As_sb[:], Am_sb[:], start=True, stop=True)
                p.mm(PS[1][0:CC, 0:CC], Am_sb[:], As_sb[:], start=True, stop=True)
                p.vcopy(A2_sb[:], PS[0][0:CC, 0:CC])
                p.vcopy(A2T_sb[:], PS[1][0:CC, 0:CC])
                p.mm(PS[0][0:CC, 0:CC], A2T_sb[:], A2_sb[:], start=True, stop=True)
                p.mm(PS[1][0:CC, 0:CC], A2_sb[:], A2T_sb[:], start=True, stop=True)
                p.vcopy(A4_sb[:], PS[0][0:CC, 0:CC])
                p.vcopy(A4T_sb[:], PS[1][0:CC, 0:CC])
                p.mm(PS[0][0:CC, 0:CC], A4T_sb[:], A4_sb[:], start=True, stop=True)
                p.mm(PS[1][0:CC, 0:CC], A4_sb[:], A4T_sb[:], start=True, stop=True)
                p.vcopy(A8_sb[:], PS[0][0:CC, 0:CC])
                p.vcopy(A8T_sb[:], PS[1][0:CC, 0:CC])
                p.mm(PS[0][0:CC, 0:CC], A8_sb[:], A8T_sb[:], start=True, stop=True)
                p.vcopy(A16_sb[:], PS[0][0:CC, 0:CC])
                p.trans(PS[3][0:CC, 0:64], sl(v_sb), id_sb[0:64, 0:64])
                p.vcopy(vt_sb[:], PS[3][0:CC, 0:64])
                p.mm(PS[4][0:CC, 0:64], sl(kbar_sb), S_sb[h][:], start=True, stop=True)
                p.tt(X_sb[:], vt_sb[:], PS[4][0:CC, 0:64], OP.subtract)
                p.mm(PS[4][0:CC, 0:64], As_sb[:], X_sb[:], start=True, stop=True)
                p.tt(X_sb[:], X_sb[:], PS[4][0:CC, 0:64], OP.subtract)
                for Ap in (A2T_sb, A4T_sb, A8T_sb, A16_sb):
                    p.mm(PS[4][0:CC, 0:64], Ap[:], X_sb[:], start=True, stop=True)
                    p.tt(X_sb[:], X_sb[:], PS[4][0:CC, 0:64], OP.add)
                p.mm(PS[5][0:CC, 0:64], sl(qbar_sb), S_sb[h][:], start=True, stop=False)
                p.mm(PS[5][0:CC, 0:64], Bs_sb[:], X_sb[:], start=False, stop=True)
                p.vcopy(ybuf_sb[:, h, :], PS[5][0:CC, 0:64])
                p.ts(X_sb[:], X_sb[:], btT_sb[:, h:h + 1], None, OP.mult)
                p.trans(PS[3][0:CC, 0:64], sl(khat_sb), id_sb[0:64, 0:64])
                p.vcopy(kht_sb[:], PS[3][0:CC, 0:64])
                p.mm(PS[4][0:64, 0:64], kht_sb[:], X_sb[:], start=True, stop=True)
                p.ts(S_sb[h][:], S_sb[h][:], eg_sb[:, h, c0 + CC - 1:c0 + CC], None, OP.mult)
                p.tt(S_sb[h][:], S_sb[h][:], PS[4][0:64, 0:64], OP.add)
            p.barrier()
            p.dma("sync", y_out[o + c0:o + c0 + CC, :].rearrange("t (h u) -> t h u", u=64),
                  ybuf_sb[:, :, :], "ST")
            p.wait("vector", "ST")
            p.barrier()

    p.wait("sync", "ST")
    return p.build()


def stage4_inmaps(I, s3res):
    cw1v = np.asarray(I["L1_attn_conv_w"])[:, 0, :]
    bsp = np.asarray(I["L1_attn_beta_spike"])[0]
    maps = []
    for c in range(8):
        b, hh = c // 2, c % 2
        cols = slice(hh * 256, (hh + 1) * 256)
        hs = slice(hh * 4, (hh + 1) * 4)
        hr = np.concatenate([s3res[2 * b]["hr_out"], s3res[2 * b + 1]["hr_out"]], axis=1)
        hr = np.concatenate([np.zeros((D, 4), np.float32), hr], axis=1)
        asp4 = np.asarray(I["L1_attn_alpha_spike"])[0][hs].T
        bad4 = np.asarray(I["L1_attn_bad"])[cols].reshape(4, 64).T
        bspsel = np.zeros((64, 4, 4), np.float32)
        sel4 = np.zeros((64, 4, 4), np.float32)
        e4h = np.zeros((4, 4, 64), np.float32)
        for hl in range(4):
            sel4[:, hl, hl] = 1.0
            bspsel[:, hl, hl] = bsp[hh * 4 + hl, :]
            e4h[hl, hl, :] = 1.0
        maps.append(dict(
            hr_s=np.ascontiguousarray(hr),
            cw1=np.ascontiguousarray(cw1v),
            cb1=np.asarray(I["L1_attn_conv_b"]).reshape(D, 1),
            wq=np.ascontiguousarray(np.asarray(I["L1_attn_Wq"])[:, cols]),
            wk=np.ascontiguousarray(np.asarray(I["L1_attn_Wk"])[:, cols]),
            wv=np.ascontiguousarray(np.asarray(I["L1_attn_Wv"])[:, cols]),
            wsp=np.ascontiguousarray(np.asarray(I["L1_attn_Wspike"])[:, cols]),
            wau=np.ascontiguousarray(I["L1_attn_Wau"]),
            bau=np.asarray(I["L1_attn_bau"]).reshape(64, 1),
            wad=np.ascontiguousarray(np.asarray(I["L1_attn_Wad"])[:, cols]),
            bad4=np.ascontiguousarray(bad4),
            wbe=np.ascontiguousarray(np.asarray(I["L1_attn_Wbeta"])[:, hs]),
            bbe=np.ascontiguousarray(np.asarray(I["L1_attn_bbeta"])[hs].reshape(4, 1)),
            asp4=np.ascontiguousarray(asp4),
            bspsel=bspsel, sel4=sel4, e4h=e4h,
            ident=np.eye(128, dtype=np.float32),
            trilS=np.ascontiguousarray(np.triu(np.ones((32, 32), np.float32), 1)),
            trilI=np.ascontiguousarray(np.triu(np.ones((32, 32), np.float32), 0)),
            trilL=np.ascontiguousarray(np.tril(np.ones((32, 32), np.float32), -1)),
        ))
    return maps



FP4_MID = (0.25, 0.75, 1.25, 1.75, 2.5, 3.5, 5.0)
FP4_STEP = (0.5, 0.5, 0.5, 0.5, 1.0, 1.0, 2.0)


def build_stage5():
    p = Prog("s5")
    nc = p.nc
    DT = D // 128
    TC = 512
    CO = chunks(0, T2, TC)
    FT = 4096 // 128
    WT = 2816 // 128

    yhT = nc.dram_tensor("yhT", [512, T2], F32, kind="ExternalInput").ap()
    xn1T = nc.dram_tensor("xn1T", [D, T2], F32, kind="ExternalInput").ap()
    xn1tok = nc.dram_tensor("xn1tok", [T2, D], F32, kind="ExternalInput").ap()
    x1T = nc.dram_tensor("x1T", [D, T2], F32, kind="ExternalInput").ap()
    hnw = nc.dram_tensor("hnw", [512, 1], F32, kind="ExternalInput").ap()
    wo = nc.dram_tensor("wo", [512, D], F32, kind="ExternalInput").ap()
    wu1 = nc.dram_tensor("wu1", [D, 64], F32, kind="ExternalInput").ap()
    bu1 = nc.dram_tensor("bu1", [64, 1], F32, kind="ExternalInput").ap()
    wu2 = nc.dram_tensor("wu2", [64, D], F32, kind="ExternalInput").ap()
    bu2 = nc.dram_tensor("bu2", [D, 1], F32, kind="ExternalInput").ap()
    ffnw = nc.dram_tensor("ffnw", [D, 1], F32, kind="ExternalInput").ap()
    wff1 = nc.dram_tensor("wff1", [D, 4096], F32, kind="ExternalInput").ap()
    wff3 = nc.dram_tensor("wff3", [D, 4096], F32, kind="ExternalInput").ap()
    wff2 = nc.dram_tensor("wff2", [4096, D], F32, kind="ExternalInput").ap()
    wg_n = nc.dram_tensor("wg_n", [2816, D], F32, kind="ExternalInput").ap()
    wu_n = nc.dram_tensor("wu_n", [2816, D], F32, kind="ExternalInput").ap()
    wd_n = nc.dram_tensor("wd_n", [D, 2816], F32, kind="ExternalInput").ap()
    sel8 = nc.dram_tensor("sel8", [128, 4, 8], F32, kind="ExternalInput").ap()
    e8 = nc.dram_tensor("e8", [8, 4, 128], F32, kind="ExternalInput").ap()
    ident = nc.dram_tensor("ident5", [128, 128], BF16, kind="ExternalInput").ap()
    ones_col = nc.dram_tensor("ones_col5", [128, 1], F32, kind="ExternalInput").ap()
    ones_row = nc.dram_tensor("ones_row5", [1, 128], F32, kind="ExternalInput").ap()
    out_t = nc.dram_tensor("out_t", [D, T2], F32, kind="ExternalOutput").ap()
    g1_d = nc.dram_tensor("g1_d", [D, T2], BF16, kind="Internal").ap()

    hnw_sb = p.sb("hnw_sb", [128, 4, 1], F32)
    bu1_sb = p.sb("bu1_sb", [64, 1], F32)
    bu2_sb = p.sb("bu2_sb", [128, DT, 1], F32)
    ffnw_sb = p.sb("ffnw_sb", [128, DT, 1], F32)
    sel8_sb = p.sb("sel8_sb", [128, 4, 8], F32)
    e8_sb = p.sb("e8_sb", [8, 4, 128], F32)
    id_sb = p.sb("id5_sb", [128, 128], BF16)
    ones_sb = p.sb("ones5", [128, 1], F32)
    onesr_sb = p.sb("onesr5", [1, 128], F32)
    cst = p.sb("cst5", [128, 2], F32)
    wgq_sb = [p.sb(f"wgq{i}", [128, 2816], BF16) for i in range(DT)]
    wuq_sb = [p.sb(f"wuq{i}", [128, 2816], BF16) for i in range(DT)]
    wdq_sb = [p.sb(f"wdq{i}", [128, D], BF16) for i in range(WT)]
    qv_sb = p.sb("qv", [128, 2816], F32)
    qm_sb = p.sb("qm", [128, 2816], BF16)
    qc_sb = p.sb("qc", [128, 2816], BF16)
    qsg_sb = p.sb("qsg", [128, 2816], BF16)
    qs_sb = p.sb("qs", [128, 176], F32)
    qr_sb = p.sb("qr", [128, 176], F32)
    qq_sb = p.sb("qq5", [128, 2816], BF16)
    xqf_sb = p.sb("xqf", [128, DT, 128], BF16)
    htok_sb = p.sb("htok", [128, 2816], F32)
    hqf_sb = p.sb("hqf", [128, WT, 128], BF16)

    PS = [p.ps(f"ps{i}", [128, 512], F32) for i in range(8)]
    for s in ("LD", "LDG", "LDS", "LDW", "ST"):
        p.sem(s)
    p.memset("vector", cst[:, 0:1], 1e-6)
    p.memset("vector", cst[:, 1:2], 1e-12)

    def col(t3, i, j):
        return t3[:, i, j:j + 1]

    p.dma("sync", id_sb[:], ident[:, :], "LD")
    p.dma("sync", hnw_sb[:], hnw.rearrange("(a p) k -> p a k", p=128), "LD", slow=True)
    p.dma("sync", bu1_sb[:], bu1[:, :], "LD")
    p.dma("sync", bu2_sb[:], bu2.rearrange("(a p) k -> p a k", p=128), "LD", slow=True)
    p.dma("sync", ffnw_sb[:], ffnw.rearrange("(a p) k -> p a k", p=128), "LD", slow=True)
    p.dma("sync", sel8_sb[:], sel8[:, :, :], "LD")
    p.dma("sync", e8_sb[:], e8[:, :, :], "LD")
    p.dma("sync", ones_sb[:], ones_col[:, :], "LD")
    p.dma("sync", onesr_sb[:], ones_row[:, :], "LD")
    for en in ("scalar", "vector", "tensor"):
        p.wait(en, "LD")
        p.wait(en, "LDG")
    p.barrier()

    def quant_rows(src_ap, ncols, dst_bf):
        nb = ncols // 16
        srcv = src_ap.rearrange("p (b u) -> p b u", u=16)
        p.vreduce(qs_sb[:, 0:nb], srcv, OP.max, absval=True)
        p.ts(qs_sb[:, 0:nb], qs_sb[:, 0:nb], 1.0 / 6.0, None, OP.mult)
        p.ts(qs_sb[:, 0:nb], qs_sb[:, 0:nb], 1e-12, None, OP.max)
        p.vrecip(qr_sb[:, 0:nb], qs_sb[:, 0:nb])
        rs_b = bass.broadcast_tensor_aps(srcv, qr_sb[:, 0:nb].rearrange("p (b u) -> p b u", u=1))[1]
        p.tt(qv_sb.rearrange("p (b u) -> p b u", u=16)[:, 0:nb, :], srcv, rs_b, OP.mult)
        p.act(qsg_sb[:, 0:ncols], qv_sb[:, 0:ncols], AF.Sign)
        p.act(qv_sb[:, 0:ncols], qv_sb[:, 0:ncols], AF.Abs)
        p.ts(qm_sb[:, 0:ncols], qv_sb[:, 0:ncols], FP4_MID[0], FP4_STEP[0], OP.is_gt, OP.mult)
        for i in (1, 2, 3, 4, 5, 6):
            p.ts(qc_sb[:, 0:ncols], qv_sb[:, 0:ncols], FP4_MID[i], FP4_STEP[i], OP.is_gt, OP.mult)
            p.tt(qm_sb[:, 0:ncols], qm_sb[:, 0:ncols], qc_sb[:, 0:ncols], OP.add)
        p.tt(qm_sb[:, 0:ncols], qm_sb[:, 0:ncols], qsg_sb[:, 0:ncols], OP.mult)
        ss_b = bass.broadcast_tensor_aps(qm_sb.rearrange("p (b u) -> p b u", u=16)[:, 0:nb, :],
                                         qs_sb[:, 0:nb].rearrange("p (b u) -> p b u", u=1))[1]
        p.tt(dst_bf.rearrange("p (b u) -> p b u", u=16)[:, 0:nb, :],
             qm_sb.rearrange("p (b u) -> p b u", u=16)[:, 0:nb, :], ss_b, OP.mult)

    def quant_weight(w_nat, nrows, ncols, dstT_tiles):
        nrt = nrows // 128
        for rt in range(nrt):
            p.dma("sync", qv_sb[:, 0:ncols], w_nat[rt * 128:(rt + 1) * 128, :], "LDS")
            p.wait("vector", "LDS")
            p.wait("scalar", "LDS")
            p.barrier()
            quant_rows(qv_sb[:, 0:ncols], ncols, qq_sb[:, 0:ncols])
            p.barrier()
            for ct in range(ncols // 128):
                p.trans(PS[7].bitcast(BF16)[:, 0:128], qq_sb[:, ct * 128:(ct + 1) * 128], id_sb[:, :])
                p.barrier()
                p.vcopy(dstT_tiles[ct][:, rt * 128:(rt + 1) * 128], PS[7].bitcast(BF16)[:, 0:128])
                p.barrier()

    quant_weight(wg_n, 2816, D, wgq_sb)
    quant_weight(wu_n, 2816, D, wuq_sb)
    quant_weight(wd_n, D, 2816, wdq_sb)

    for tt in range(T2 // 128):
        r = slice(tt * 128, (tt + 1) * 128)
        p.dma("sync", qv_sb[:, 0:D], xn1tok[r, :], "LDS")
        p.wait("vector", "LDS")
        p.wait("scalar", "LDS")
        p.barrier()
        quant_rows(qv_sb[:, 0:D], D, qq_sb[:, 0:D])
        p.barrier()
        for ct in range(DT):
            p.trans(PS[7].bitcast(BF16)[:, 0:128], qq_sb[:, ct * 128:(ct + 1) * 128], id_sb[:, :])
            p.barrier()
            p.vcopy(xqf_sb[:, ct, :], PS[7].bitcast(BF16)[:, 0:128])
            p.barrier()
        for (wsrc, isgate) in ((wgq_sb, True), (wuq_sb, False)):
            for (co_, wd_) in chunks(0, 2816, 512):
                for kk in range(DT):
                    p.mm(PS[0][:, 0:wd_], xqf_sb[:, kk, :], wsrc[kk][:, co_:co_ + wd_],
                         start=(kk == 0), stop=(kk == DT - 1))
                p.barrier()
                if isgate:
                    p.act(htok_sb[:, co_:co_ + wd_], PS[0][:, 0:wd_], AF.Silu)
                else:
                    p.vcopy(qv_sb[:, co_:co_ + wd_], PS[0][:, 0:wd_])
                p.barrier()
        p.tt(htok_sb[:, 0:2816], htok_sb[:, 0:2816], qv_sb[:, 0:2816], OP.mult)
        p.barrier()
        quant_rows(htok_sb[:, 0:2816], 2816, qq_sb[:, 0:2816])
        p.barrier()
        for ct in range(WT):
            p.trans(PS[7].bitcast(BF16)[:, 0:128], qq_sb[:, ct * 128:(ct + 1) * 128], id_sb[:, :])
            p.barrier()
            p.vcopy(hqf_sb[:, ct, :], PS[7].bitcast(BF16)[:, 0:128])
            p.barrier()
        for mt in range(DT):
            for kk in range(WT):
                p.mm(PS[1][:, 0:128], wdq_sb[kk][:, mt * 128:(mt + 1) * 128], hqf_sb[:, kk, :],
                     start=(kk == 0), stop=(kk == WT - 1))
            p.barrier()
            p.act(qq_sb[:, 0:128], PS[1][:, 0:128], AF.Silu)
            p.barrier()
            p.dma("sync", g1_d[mt * 128:(mt + 1) * 128, r], qq_sb[:, 0:128], "ST")
            p.wait("scalar", "ST")
            p.barrier()

    p.barrier()
    p.sb_free("hqf", "htok", "xqf", "qq5", "qr", "qs", "qsg", "qc", "qm", "qv",
              *[f"wdq{i}" for i in reversed(range(WT))],
              *[f"wuq{i}" for i in reversed(range(DT))],
              *[f"wgq{i}" for i in reversed(range(DT))])

    woq_sb = [p.sb(f"woq{i}", [128, D], BF16) for i in range(4)]
    wu1_sb = [p.sb(f"wu1_{i}", [128, 64], BF16) for i in range(DT)]
    wu2_sb = p.sb("wu2_sb", [64, D], BF16)
    g1c_sb = p.sb("g1c", [128, DT, TC], BF16)
    for i in range(4):
        p.dma("gpsimd", woq_sb[i][:], wo[i * 128:(i + 1) * 128, :], "LDG")
    for i in range(DT):
        p.dma("gpsimd", wu1_sb[i][:], wu1[i * 128:(i + 1) * 128, :], "LDG")
    p.dma("gpsimd", wu2_sb[:], wu2[:, :], "LDG")
    for en in ("scalar", "vector", "tensor"):
        p.wait(en, "LDG")
    p.barrier()
    yh_sb = p.sb("yh5", [128, 4, TC], F32)
    yhn_sb = p.sb("yhn5", [128, 4, TC], BF16)
    xn1c_sb = p.sb("xn1c", [128, DT, TC], F32)
    xn1b_sb = p.sb("xn1b5", [128, DT, TC], BF16)
    x1c_sb = p.sb("x1c", [128, TC], F32)
    u1_sb = p.sb("u1", [64, TC], BF16)
    gat_sb = p.sb("gat", [128, TC], F32)
    y1c_sb = p.sb("y1c", [128, DT, TC], F32)
    y1bc_sb = p.sb("y1bc", [128, DT, TC], BF16)
    zc_sb = p.sb("zc", [128, DT, TC], BF16)
    hsw_sb = p.sb("hsw", [128, FT, TC], BF16)
    w1s_sb = [p.sb(f"w1s{i}", [128, DT * 128], BF16) for i in range(2)]
    w3s_sb = [p.sb(f"w3s{i}", [128, DT * 128], BF16) for i in range(2)]
    w2s_sb = [p.sb(f"w2s{i}", [128, FT * 128], BF16) for i in range(2)]
    t1_sb = p.sb("t1", [128, TC], F32)
    t2_sb = p.sb("t2", [128, TC], F32)
    rstd_sb = p.sb("rstd5", [1, TC], F32)
    trow_sb = p.sb("trow5", [1, TC], F32)

    for ci, (o, w) in enumerate(CO):
        for i in range(4):
            p.dma("sync", yh_sb[:, i, 0:w], yhT[i * 128:(i + 1) * 128, o:o + w], "LDS")
        for i in range(DT):
            p.dma("sync", xn1c_sb[:, i, 0:w], xn1T[i * 128:(i + 1) * 128, o:o + w], "LDS")
            p.dma("sync", g1c_sb[:, i, 0:w], g1_d[i * 128:(i + 1) * 128, o:o + w], "LDS")
            p.dma("gpsimd", xn1b_sb[:, i, 0:w], xn1T[i * 128:(i + 1) * 128, o:o + w], "LDG")
        p.wait("scalar", "LDS")
        p.wait("vector", "LDS")
        p.wait("tensor", "LDS")
        p.wait("tensor", "LDG")
        p.barrier()
        for i in range(4):
            p.act(t1_sb[:, 0:w], yh_sb[:, i, 0:w], AF.Square)
            p.barrier()
            p.mm(PS[2][0:8, 0:w], sel8_sb[:, i, :], t1_sb[:, 0:w], start=(i == 0), stop=(i == 3))
        p.barrier()
        p.ts(t1_sb[0:8, 0:w], PS[2][0:8, 0:w], 1.0 / 64.0)
        p.barrier()
        p.act(t2_sb[0:8, 0:w], t1_sb[0:8, 0:w], AF.Sqrt, bias=cst[0:8, 0:1])
        p.barrier()
        p.vrecip(t1_sb[0:8, 0:w], t2_sb[0:8, 0:w])
        p.barrier()
        for i in range(4):
            p.mm(PS[2][:, 0:w], e8_sb[:, i, :], t1_sb[0:8, 0:w], start=True, stop=True)
            p.barrier()
            p.tt(t2_sb[:, 0:w], yh_sb[:, i, 0:w], PS[2][:, 0:w], OP.mult)
            p.barrier()
            p.ts(yhn_sb[:, i, 0:w], t2_sb[:, 0:w], col(hnw_sb, i, 0), None, OP.mult)
            p.barrier()
        for kk in range(DT):
            p.mm(PS[3][0:64, 0:w], wu1_sb[kk][:], xn1b_sb[:, kk, 0:w], start=(kk == 0), stop=(kk == DT - 1))
        p.barrier()
        p.act(u1_sb[:, 0:w], PS[3][0:64, 0:w], AF.Silu, bias=bu1_sb[:, 0:1])
        p.barrier()
        for dt in range(DT):
            for kk in range(4):
                p.mm(PS[0][:, 0:w], woq_sb[kk][:, dt * 128:(dt + 1) * 128], yhn_sb[:, kk, 0:w],
                     start=(kk == 0), stop=(kk == 3))
            p.mm(PS[1][:, 0:w], wu2_sb[:, dt * 128:(dt + 1) * 128], u1_sb[:, 0:w], start=True, stop=True)
            p.barrier()
            p.act(gat_sb[:, 0:w], PS[1][:, 0:w], AF.Sigmoid, bias=col(bu2_sb, dt, 0))
            p.barrier()
            p.tt(t1_sb[:, 0:w], PS[0][:, 0:w], gat_sb[:, 0:w], OP.mult)
            p.barrier()
            p.tt(y1c_sb[:, dt, 0:w], xn1c_sb[:, dt, 0:w], t1_sb[:, 0:w], OP.add)
            p.barrier()
            p.vcopy(y1bc_sb[:, dt, 0:w], y1c_sb[:, dt, 0:w])
        p.barrier()
        for i in range(DT):
            p.act(t1_sb[:, 0:w], y1c_sb[:, i, 0:w], AF.Square)
            p.barrier()
            p.mm(PS[4][0:1, 0:w], ones_sb[:], t1_sb[:, 0:w], start=(i == 0), stop=(i == DT - 1),
                 skip_group_check=True)
            p.barrier()
        p.ts(rstd_sb[0:1, 0:w], PS[4][0:1, 0:w], 1.0 / D)
        p.barrier()
        p.act(trow_sb[0:1, 0:w], rstd_sb[0:1, 0:w], AF.Sqrt, bias=cst[0:1, 0:1])
        p.barrier()
        p.vrecip(rstd_sb[0:1, 0:w], trow_sb[0:1, 0:w])
        p.barrier()
        p.mm(PS[4][:, 0:w], onesr_sb[:], rstd_sb[0:1, 0:w], start=True, stop=True)
        p.barrier()
        for i in range(DT):
            p.tt(t1_sb[:, 0:w], y1c_sb[:, i, 0:w], PS[4][:, 0:w], OP.mult)
            p.barrier()
            p.ts(zc_sb[:, i, 0:w], t1_sb[:, 0:w], col(ffnw_sb, i, 0), None, OP.mult)
            p.barrier()
        for mt in range(FT):
            bufm = mt % 2
            for kk in range(DT):
                p.dma("gpsimd", w1s_sb[bufm][:, kk * 128:(kk + 1) * 128],
                      wff1[kk * 128:(kk + 1) * 128, mt * 128:(mt + 1) * 128], "LDW")
                p.dma("gpsimd", w3s_sb[bufm][:, kk * 128:(kk + 1) * 128],
                      wff3[kk * 128:(kk + 1) * 128, mt * 128:(mt + 1) * 128], "LDW")
            p.wait("tensor", "LDW", p.tgt("LDW"))
            p.barrier()
            for kk in range(DT):
                p.mm(PS[0][:, 0:w], w1s_sb[bufm][:, kk * 128:(kk + 1) * 128], zc_sb[:, kk, 0:w],
                     start=(kk == 0), stop=(kk == DT - 1))
            for kk in range(DT):
                p.mm(PS[1][:, 0:w], w3s_sb[bufm][:, kk * 128:(kk + 1) * 128], zc_sb[:, kk, 0:w],
                     start=(kk == 0), stop=(kk == DT - 1))
            p.barrier()
            p.act(t1_sb[:, 0:w], PS[0][:, 0:w], AF.Silu)
            p.barrier()
            p.tt(t2_sb[:, 0:w], t1_sb[:, 0:w], PS[1][:, 0:w], OP.mult)
            p.barrier()
            p.vcopy(hsw_sb[:, mt, 0:w], t2_sb[:, 0:w])
            p.barrier()
        for dt in range(DT):
            bufm = dt % 2
            for kk in range(FT):
                p.dma("gpsimd", w2s_sb[bufm][:, kk * 128:(kk + 1) * 128],
                      wff2[kk * 128:(kk + 1) * 128, dt * 128:(dt + 1) * 128], "LDW")
            p.wait("tensor", "LDW", p.tgt("LDW"))
            p.dma("sync", x1c_sb[:, 0:w], x1T[dt * 128:(dt + 1) * 128, o:o + w], "LDS")
            p.wait("vector", "LDS")
            p.barrier()
            for kk in range(FT):
                p.mm(PS[0][:, 0:w], w2s_sb[bufm][:, kk * 128:(kk + 1) * 128], hsw_sb[:, kk, 0:w],
                     start=(kk == 0), stop=(kk == FT - 1))
            p.barrier()
            p.tt(t1_sb[:, 0:w], y1bc_sb[:, dt, 0:w], PS[0][:, 0:w], OP.add)
            p.barrier()
            p.tt(t1_sb[:, 0:w], t1_sb[:, 0:w], g1c_sb[:, dt, 0:w], OP.mult)
            p.barrier()
            p.tt(t1_sb[:, 0:w], x1c_sb[:, 0:w], t1_sb[:, 0:w], OP.add)
            p.barrier()
            p.dma("sync", out_t[dt * 128:(dt + 1) * 128, o:o + w], t1_sb[:, 0:w], "ST")
            p.wait("vector", "ST")
            p.barrier()

    p.wait("sync", "ST")
    return p.build()


def stage5_inmaps(I, s3res, s4res):
    sel8 = np.zeros((128, 4, 8), np.float32)
    e8 = np.zeros((8, 4, 128), np.float32)
    for h in range(8):
        pt, rb = h // 2, 64 * (h % 2)
        sel8[rb:rb + 64, pt, h] = 1.0
        e8[h, pt, rb:rb + 64] = 1.0
    shared = dict(
        hnw=np.asarray(I["L1_attn_head_norm_w"]).reshape(512, 1),
        wo=np.ascontiguousarray(I["L1_attn_Wo"]),
        wu1=np.ascontiguousarray(I["L1_attn_Wu1"]),
        bu1=np.asarray(I["L1_attn_bu1"]).reshape(64, 1),
        wu2=np.ascontiguousarray(I["L1_attn_Wu2"]),
        bu2=np.asarray(I["L1_attn_bu2"]).reshape(D, 1),
        ffnw=np.asarray(I["L1_attn_ff_norm_w"]).reshape(D, 1),
        wff1=np.ascontiguousarray(I["L1_attn_Wff1"]),
        wff3=np.ascontiguousarray(I["L1_attn_Wff3"]),
        wff2=np.ascontiguousarray(I["L1_attn_Wff2"]),
        wg_n=np.ascontiguousarray(I["L1_ffn_w_gate"]),
        wu_n=np.ascontiguousarray(I["L1_ffn_w_up"]),
        wd_n=np.ascontiguousarray(I["L1_ffn_w_down"]),
        sel8=sel8, e8=e8,
        ident5=np.eye(128, dtype=np.float32).astype(BF),
        ones_col5=np.ones((128, 1), np.float32),
        ones_row5=np.ones((1, 128), np.float32),
    )
    maps = []
    for c in range(8):
        b, j = c // 2, c % 2
        sl = slice(j * T2, (j + 1) * T2)
        yh = np.concatenate([s4res[2 * b]["y_out"], s4res[2 * b + 1]["y_out"]], axis=1)[sl]
        m = dict(shared)
        m["yhT"] = _t(yh)
        m["xn1T"] = np.ascontiguousarray(s3res[c]["xn1_out"])
        m["xn1tok"] = _t(s3res[c]["xn1_out"])
        m["x1T"] = np.ascontiguousarray(s3res[c]["x1_out"])
        maps.append(m)
    return maps



def _t(a):
    return np.ascontiguousarray(np.asarray(a).T)


def stage1_inmaps(I):
    x = np.asarray(I["L0_attn_resid"])
    shared = dict(
        in_wT=_t(I["L0_attn_in_w"]),
        skip_wT=_t(I["L0_attn_skip_w"]),
        wpackT=np.ascontiguousarray(
            np.concatenate([_t(I["L0_attn_wd1"]), _t(I["L0_attn_wB"]), _t(I["L0_attn_wC"])], axis=1)),
        wd2T=_t(I["L0_attn_wd2"]),
        conv_w=np.ascontiguousarray(I["L0_attn_conv_w"]),
        conv_b=np.asarray(I["L0_attn_conv_b"]).reshape(E, 1),
        wd2_b=np.asarray(I["L0_attn_wd2_b"]).reshape(E, 1),
        norm_w=np.asarray(I["L0_attn_norm_w"]).reshape(D, 1),
        ones_col=np.ones((128, 1), np.float32),
        ones_row=np.ones((1, 128), np.float32),
    )
    maps = []
    for c in range(8):
        b, j = c // 2, c % 2
        t0 = j * T2
        xs = np.zeros((T2H, D), np.float32)
        lo = max(0, t0 - HALO)
        xs[HALO - (t0 - lo):] = x[b, lo:t0 + T2]
        m = dict(shared)
        m["xT"] = _t(xs)
        maps.append(m)
    return maps


def kernel(**inputs):
    I = inputs
    nc1 = build_stage1()
    r1 = run_bass_kernel_spmd(nc1, stage1_inmaps(I), core_ids=list(range(8))).results
    nc2 = build_stage2()
    r2 = run_bass_kernel_spmd(nc2, stage2_inmaps(I, r1), core_ids=list(range(8))).results
    nc3 = build_stage3()
    r3 = run_bass_kernel_spmd(nc3, stage3_inmaps(I, r1, r2), core_ids=list(range(8))).results
    nc4 = build_stage4()
    r4 = run_bass_kernel_spmd(nc4, stage4_inmaps(I, r3), core_ids=list(range(8))).results
    nc5 = build_stage5()
    r5 = run_bass_kernel_spmd(nc5, stage5_inmaps(I, r3, r4), core_ids=list(range(8))).results
    out = np.empty((B, T, D), np.float32)
    for c in range(8):
        b, j = c // 2, c % 2
        out[b, j * T2:(j + 1) * T2, :] = r5[c]["out_t"].T
    return out
